# version 10
# baseline (speedup 1.0000x reference)
import numpy as np
import ml_dtypes
import concourse.bass as bass
import concourse.mybir as mybir
from concourse.bass_utils import run_bass_kernel_spmd

F32 = mybir.dt.float32
BF16 = mybir.dt.bfloat16
I32 = mybir.dt.int32
Alu = mybir.AluOpType
Act = mybir.ActivationFunctionType

NCORES = 8
T = 16384
TC = 2048
W = 256
TT = TC + W
C = 1024
NCHB = 8
H = 16
N = 64
L = 128
NCH = TT // L
NWARM = W // L
E = 1228
EP = 1280
NEB = 10
DL = 128
DECAY_C = 0.7408182206817179

A_TILES = [(0, 512), (512, 512), (1024, 512), (1536, 512), (2048, 256)]
P_TILES = [(256, 512), (768, 512), (1280, 512), (1792, 512)]
R_GROUPS = [(2 * i, 2) for i in range(9)]

DEBUG = False
STOP_AFTER = "ALL"
kernel_SERIAL_A1 = False
R1_C = 2
R1_H = 0

_uniq = [0]


def uname(n):
    _uniq[0] += 1
    return f"{n}_{_uniq[0]}"


class K:
    ENG = ("T", "V", "A", "G", "S")
    DMAC = ("LD", "LD2", "SD")

    def __init__(self, nc):
        self.nc = nc
        self.instrs = {e: [] for e in self.ENG}
        self.cnt = {e: 0 for e in self.ENG}
        self.unit = {e: 1 for e in self.ENG}
        for c in self.DMAC:
            self.cnt[c] = 0
            self.unit[c] = 16
        self.sems = {}

    def make_sems(self, stack):
        for name in self.ENG + self.DMAC:
            self.sems[name] = stack.enter_context(self.nc.semaphore("sem" + name))

    def emit(self, e, fn, deps=(), inc=True, dma=None):
        deps = tuple((dk, self.cnt[dk]) if dk in self.DMAC else (dk, dv)
                     for (dk, dv) in (d for d in deps if d is not None))
        self.instrs[e].append((fn, deps, inc, dma))
        if dma is not None:
            self.cnt[dma] += 1
            return (dma, self.cnt[dma])
        if inc:
            self.cnt[e] += 1
            return (e, self.cnt[e])
        return None

    def engtok(self, *engs):
        return [(e, self.cnt[e]) for e in engs]

    def barrier(self):
        return [(e, self.cnt[e]) for e in ("T", "V", "A", "G")] + \
               [(c, self.cnt[c]) for c in self.DMAC]

    def realize(self):
        nc = self.nc
        sems = self.sems
        unit = self.unit
        self.trace = {e: [] for e in self.ENG}

        def run(engine, key):
            waited = {}
            for fn, deps, inc, dma in self.instrs[key]:
                for (dk, dv) in deps:
                    if dv <= 0 or dk == key:
                        continue
                    if waited.get(dk, 0) < dv:
                        engine.wait_ge(sems[dk], dv * unit[dk])
                        waited[dk] = dv
                        self.trace[key].append(("W", dk, dv))
                inst = fn(engine)
                self.trace[key].append(("I", dma or "", inc))
                if dma is not None:
                    inst.then_inc(sems[dma], 16)
                elif inc:
                    inst.then_inc(sems[key], 1)

        with nc.Block() as block:
            @block.tensor
            def _(t):
                run(t, "T")

            @block.vector
            def _(v):
                run(v, "V")

            @block.scalar
            def _(a):
                run(a, "A")

            @block.gpsimd
            def _(g):
                run(g, "G")
                g.wait_ge(sems["SD"], self.cnt["SD"] * 16)

            @block.sync
            def _(s):
                run(s, "S")
                s.wait_ge(sems["LD"], self.cnt["LD"] * 16)
                s.wait_ge(sems["LD2"], self.cnt["LD2"] * 16)


def build_nc():
    nc = bass.Bass("TRN2", target_bir_lowering=False)
    k = K(nc)
    build_nc.k = k

    xT = nc.dram_tensor("xT", [C, TT], F32, kind="ExternalInput")
    wmask = nc.dram_tensor("wmask", [128, W], F32, kind="ExternalInput")
    out_t = nc.dram_tensor("out", [C, TC], F32, kind="ExternalOutput")

    WNAMES = [
        ("rW1", [C, EP]), ("rW2", [EP, C]), ("kW1", [C, EP]), ("kW2", [EP, C]),
        ("vW1", [C, EP]), ("vW2", [EP, C]), ("oW1", [C, EP]), ("oW2", [EP, C]),
        ("gA", [C, DL]), ("gB", [DL, C]), ("aA", [C, DL]), ("aB", [DL, C]),
        ("dA", [C, DL]), ("dB", [DL, C]), ("fWT", [C, C]),
    ]
    wt = {n: nc.dram_tensor(n, s, F32, kind="ExternalInput") for n, s in WNAMES}
    PNAMES_C = ["mu_r", "mu_k", "mu_v", "mu_g", "mu_a", "mu_d", "rb2", "kb2",
                "vb2", "ob2", "ab", "db", "rem_mult", "iclr_mix", "bonus_mult",
                "ln_g", "ln_b", "fbias"]
    PNAMES_E = ["rb1", "kb1", "vb1", "ob1"]
    pt = {n: nc.dram_tensor(n, [C], F32, kind="ExternalInput") for n in PNAMES_C}
    pt |= {n: nc.dram_tensor(n, [EP], F32, kind="ExternalInput") for n in PNAMES_E}
    scale_t = nc.dram_tensor("act_scale", [1, 1], F32, kind="ExternalInput")
    epscol_t = nc.dram_tensor("epscol", [128, 4], F32, kind="ExternalInput")
    CONSTS = [("idn64", [128, 64], BF16), ("idn128", [128, 128], BF16),
              ("mSLn", [128, 128], F32), ("mSUn", [128, 128], F32),
              ("mSU", [128, 128], F32), ("mUIn", [128, 128], F32),
              ("mUI", [128, 128], F32),
              ("bones", [128, 128], BF16), ("ones128", [128, 128], BF16)]
    ct = {n: nc.dram_tensor(n, s, d, kind="ExternalInput") for n, s, d in CONSTS}

    def streams(name, width, dtype=BF16):
        return [nc.dram_tensor(f"{name}{i}", [128, width], dtype) for i in range(NCHB)]

    kst_h = streams("k_h", TT)
    vst_h = streams("v_h", TT)
    a_h = streams("a_h", TT)
    lcl_h = streams("lcl_h", TT, F32)
    ld_h = streams("ld_h", TT)
    r_h = streams("r_h", TC)
    gate_h = streams("gate_h", TC)
    Af_h = streams("Af_h", TT)
    Kc_h = streams("Kc_h", TT)
    Vc_h = streams("Vc_h", TT)
    Wv_h = streams("Wv_h", TT)
    Wkn_h = streams("Wkn_h", TT)
    RK_h = streams("RK_h", TT)
    cfm_h = streams("cfm_h", TC)
    bon_h = streams("bon_h", TC)
    mo_h = streams("mo_h", TC)

    dbg = {}
    if DEBUG:
        for nm, st in (("kk", kst_h), ("vv", vst_h), ("aa", a_h),
                       ("rr", r_h), ("Af", Af_h), ("Kc", Kc_h), ("RK", RK_h),
                       ("cf", cfm_h), ("bo", bon_h), ("mo", mo_h), ("ga", gate_h),
                       ("lc", lcl_h), ("Wc", Wv_h), ("Wk", Wkn_h),
                       ("Vc", Vc_h)):
            wdt = st[0].shape[1]
            dbg[nm] = (nc.dram_tensor("dbg_" + nm, [C, wdt], F32,
                                      kind="ExternalOutput"), st)

    from contextlib import ExitStack
    stack = ExitStack()
    k.make_sems(stack)

    def sb(name, shape, dtype):
        return stack.enter_context(nc.sbuf_tensor(name, shape, dtype))

    pb = [stack.enter_context(nc.psum_tensor(f"pb{i}", [128, 512], F32))
          for i in range(8)]

    cs = {}
    for n, s, d in CONSTS:
        cs[n] = sb("c_" + n, s, d)
        k.emit("S", lambda e, t=ct[n], b=cs[n]: e.dma_start(b[:, :], t[:, :]),
               dma="LD")
    wm_s = sb("c_wmask", [128, W], F32)
    k.emit("S", lambda e: e.dma_start(wm_s[:, :], wmask[:, :]), dma="LD")

    pcols = {}
    for n in PNAMES_C:
        pcols[n] = sb("p_" + n, [128, NCHB], F32)
        k.emit("S", lambda e, t=pt[n], b=pcols[n]: e.dma_start(
            b[:, :], bass.AP(t, 0, [[1, 128], [128, NCHB]]),
            allow_slow_non_contiguous=True), dma="LD")
    for n in PNAMES_E:
        pcols[n] = sb("p_" + n, [128, NEB], F32)
        k.emit("S", lambda e, t=pt[n], b=pcols[n]: e.dma_start(
            b[:, :], bass.AP(t, 0, [[1, 128], [128, NEB]]),
            allow_slow_non_contiguous=True), dma="LD")
    sc_s = sb("p_scale", [1, 1], F32)
    k.emit("S", lambda e: e.dma_start(sc_s[:, :], scale_t[:, :]), dma="LD")
    ec_s = sb("p_epscol", [128, 4], F32)
    k.emit("S", lambda e: e.dma_start(ec_s[:, :], epscol_t[:, :]), dma="LD")
    consts_ready = k.barrier()
    pcols["one_m_mix"] = sb("p_omm", [128, NCHB], F32)
    t_omm = k.emit("V", lambda e: e.tensor_scalar(
        pcols["one_m_mix"][:, :], pcols["iclr_mix"][:, :], -1.0, 1.0,
        Alu.mult, Alu.add), deps=consts_ready)

    bf_mu = sb("bf_mu", [128, NCHB], BF16)
    cvt = sb("cvt", [128, 512], BF16)
    zz_s = sb("zz_s", [128, 128], F32)
    cL_s = sb("cL_s", [128, NCHB * NCH], F32)

    from contextlib import ExitStack as _ES
    stackQ = _ES()
    q_s = stackQ.enter_context(nc.sbuf_tensor(uname("q_s"), [128, NCHB * TT], BF16))

    def q_sl(chb, t0, tl):
        return q_s[:, chb * TT + t0: chb * TT + t0 + tl]

    with nc.sbuf_tensor(uname("xa"), [128, NCHB * 512], F32) as xa, \
         nc.sbuf_tensor(uname("xb"), [128, NCHB * 512], BF16) as xb, \
         nc.sbuf_tensor(uname("x2b"), [128, NCHB * 512], BF16) as x2b, \
         nc.sbuf_tensor(uname("stm"), [128, 512], F32) as stm, \
         nc.sbuf_tensor(uname("str1"), [128, 512], F32) as strr:
        prev_ev = None
        for (t0, tl) in A_TILES:
            lds = []
            _bar = k.barrier() if kernel_SERIAL_A1 else None
            for chb in range(NCHB):
                lds.append(k.emit("S", lambda e, c=chb, t0=t0, tl=tl: e.dma_start(
                    xa[:, c * 512: c * 512 + tl],
                    bass.AP(xT, c * 128 * TT + t0, [[TT, 128], [1, tl]])),
                    deps=consts_ready + [prev_ev] + (_bar or []), dma="LD"))
            casts, sqs = [], []
            for chb in range(NCHB):
                casts.append(k.emit("V", lambda e, c=chb, tl=tl: e.tensor_copy(
                    xb[:, c * 512: c * 512 + tl], xa[:, c * 512: c * 512 + tl]),
                    deps=[lds[chb], prev_ev]))
                sqs.append(k.emit("V", lambda e, c=chb, tl=tl: e.tensor_mul(
                    x2b[:, c * 512: c * 512 + tl], xb[:, c * 512: c * 512 + tl],
                    xb[:, c * 512: c * 512 + tl]), deps=[casts[chb]]))
            mm1 = mm2 = None
            for chb in range(NCHB):
                mm1 = k.emit("T", lambda e, c=chb, tl=tl: e.matmul(
                    pb[6][:, 0:tl], cs["ones128"][:, :],
                    xb[:, c * 512: c * 512 + tl],
                    start=(c == 0), stop=(c == NCHB - 1)),
                    deps=[casts[chb], prev_ev])
            for chb in range(NCHB):
                mm2 = k.emit("T", lambda e, c=chb, tl=tl: e.matmul(
                    pb[7][:, 0:tl], cs["ones128"][:, :],
                    x2b[:, c * 512: c * 512 + tl],
                    start=(c == 0), stop=(c == NCHB - 1)), deps=[sqs[chb]])
            e1 = k.emit("V", lambda e, tl=tl: e.tensor_scalar(
                stm[:, 0:tl], pb[6][:, 0:tl], 1.0 / C, None, Alu.mult), deps=[mm1])
            e2 = k.emit("V", lambda e, tl=tl: e.tensor_mul(
                strr[:, 0:tl], stm[:, 0:tl], stm[:, 0:tl]), deps=[e1])
            e3 = k.emit("V", lambda e, tl=tl: e.scalar_tensor_tensor(
                strr[:, 0:tl], pb[7][:, 0:tl], 1.0 / C, strr[:, 0:tl],
                Alu.mult, Alu.subtract), deps=[mm2, e2])
            e4 = k.emit("A", lambda e, tl=tl: e.activation(
                strr[:, 0:tl], strr[:, 0:tl], Act.Ln, bias=ec_s[:, 1:2], scale=1.0),
                deps=[e3])
            e5 = k.emit("A", lambda e, tl=tl: e.activation(
                strr[:, 0:tl], strr[:, 0:tl], Act.Exp, bias=ec_s[:, 0:1], scale=-0.5),
                deps=[e4])
            for chb in range(NCHB):
                s1 = k.emit("V", lambda e, c=chb, tl=tl: e.tensor_sub(
                    xb[:, c * 512: c * 512 + tl], xb[:, c * 512: c * 512 + tl],
                    stm[:, 0:tl]), deps=[e5])
                prev_ev = k.emit("V", lambda e, c=chb, t0=t0, tl=tl: e.tensor_mul(
                    q_sl(c, t0, tl), xb[:, c * 512: c * 512 + tl], strr[:, 0:tl]),
                    deps=[s1])
        a1_done = k.barrier()

    if DEBUG:
        q_d = nc.dram_tensor("dbg_q", [C, TT], F32, kind="ExternalOutput")
        with nc.sbuf_tensor(uname("qdf"), [128, TT], F32) as qdf:
            pq = None
            for chb in range(NCHB):
                cv = k.emit("V", lambda e, c=chb: e.tensor_copy(
                    qdf[:, :], q_s[:, c * TT:(c + 1) * TT]), deps=a1_done + [pq])
                pq = k.emit("G", lambda e, c=chb: e.dma_start(
                    bass.AP(q_d, c * 128 * TT, [[TT, 128], [1, TT]]), qdf[:, :]),
                    deps=[cv], dma="SD")
            qdump_done = k.barrier()
        if STOP_AFTER == "A1":
            k.realize()
            stackQ.close()
            stack.close()
            return nc

    mu_state = {"tok": None}

    def set_mu(name, deps):
        mu_state["tok"] = k.emit("V", lambda e, nm=name: e.tensor_copy(
            bf_mu[:, :], pcols[nm][:, :]), deps=deps)

    def mlp(dst_h, W1d, b1col, W2d, b2col, tok_range, deps0):
        dst_off = TT - dst_h[0].shape[1]
        with nc.sbuf_tensor(uname("w1b"), [128, NCHB * EP], BF16) as w1b, \
             nc.sbuf_tensor(uname("w2b"), [128, NEB * C], BF16) as w2b, \
             nc.sbuf_tensor(uname("wld"), [128, EP], F32) as wld, \
             nc.sbuf_tensor(uname("b1c"), [128, NEB], F32) as b1c, \
             nc.sbuf_tensor(uname("hb"), [128, NEB * 512], BF16) as hb, \
             nc.sbuf_tensor(uname("ob"), [128, NCHB * 512], BF16) as ob:
            w1tok = []
            for kc in range(NCHB):
                ld = k.emit("S", lambda e, kc=kc, W1d=W1d: e.dma_start(
                    wld[:, 0:EP], bass.AP(W1d, kc * 128 * EP, [[EP, 128], [1, EP]])),
                    deps=list(deps0) + ([w1tok[-1]] if w1tok else []), dma="LD")
                w1tok.append(k.emit("V", lambda e, kc=kc: e.tensor_copy(
                    w1b[:, kc * EP: kc * EP + EP], wld[:, 0:EP]), deps=[ld]))
            mmt = None
            for ec in range(NEB):
                for kc in range(NCHB):
                    mmt = k.emit("T", lambda e, kc=kc, ec=ec,
                                 st=(ec == 0 and kc == 0),
                                 sp=(ec == NEB - 1 and kc == NCHB - 1): e.matmul(
                        pb[5][:, ec: ec + 1],
                        w1b[:, kc * EP + ec * 128: kc * EP + ec * 128 + 128],
                        bf_mu[:, kc: kc + 1], start=st, stop=sp),
                        deps=[w1tok[kc], mu_state["tok"]])
            badd = k.emit("V", lambda e, b1col=b1col: e.tensor_tensor(
                b1c[:, :], pb[5][:, 0:NEB], b1col[:, :], Alu.add), deps=[mmt])
            w2tok = []
            for ec in range(NEB):
                ld = k.emit("S", lambda e, ec=ec, W2d=W2d: e.dma_start(
                    wld[:, 0:C], bass.AP(W2d, ec * 128 * C, [[C, 128], [1, C]])),
                    deps=[w1tok[-1]] + ([w2tok[-1]] if w2tok else []), dma="LD")
                w2tok.append(k.emit("V", lambda e, ec=ec: e.tensor_copy(
                    w2b[:, ec * C: ec * C + C], wld[:, 0:C]), deps=[ld]))
            tstart = tok_range[0]
            tiles = [(max(t0, tstart), t0 + tl - max(t0, tstart))
                     for (t0, tl) in A_TILES if t0 + tl > tstart]
            bank_ev = {0: None, 1: None, 2: None, 3: None}
            prev_store = None
            tile_store = None
            for (t0, tl) in tiles:
                htok = [None] * NEB
                for ec in range(NEB):
                    bk = ec % 2
                    mmt = None
                    for kc in range(NCHB):
                        mmt = k.emit("T", lambda e, kc=kc, ec=ec, t0=t0, tl=tl,
                                     bk=bk, st=(kc == 0): e.matmul(
                            pb[bk][:, 0:tl],
                            w1b[:, kc * EP + ec * 128: kc * EP + ec * 128 + 128],
                            q_sl(kc, t0, tl), start=st, stop=(kc == NCHB - 1)),
                            deps=[w1tok[kc], badd, bank_ev[bk]])
                    htok[ec] = k.emit("A", lambda e, ec=ec, tl=tl, bk=bk:
                                      e.activation(
                        hb[:, ec * 512: ec * 512 + tl], pb[bk][:, 0:tl], Act.Gelu,
                        bias=b1c[:, ec: ec + 1], scale=1.0), deps=[mmt])
                    bank_ev[bk] = htok[ec]
                for cc in range(NCHB):
                    bk = 2 + cc % 2
                    mmt = None
                    for ec in range(NEB):
                        mmt = k.emit("T", lambda e, ec=ec, cc=cc, tl=tl, bk=bk,
                                     st=(ec == 0): e.matmul(
                            pb[bk][:, 0:tl],
                            w2b[:, ec * C + cc * 128: ec * C + cc * 128 + 128],
                            hb[:, ec * 512: ec * 512 + tl],
                            start=st, stop=(ec == NEB - 1)),
                            deps=[htok[ec], w2tok[ec], bank_ev[bk]])
                    ev = k.emit("V", lambda e, cc=cc, tl=tl, bk=bk, b2col=b2col:
                                e.tensor_scalar(
                        ob[:, cc * 512: cc * 512 + tl], pb[bk][:, 0:tl],
                        b2col[:, cc: cc + 1],
                        None, Alu.add),
                        deps=[mmt, tile_store if cc == 0 else None])
                    bank_ev[bk] = ev
                    prev_store = k.emit("G", lambda e, cc=cc, t0=t0, tl=tl,
                                        dst=dst_h, off=dst_off: e.dma_start(
                        dst[cc][:, t0 - off: t0 - off + tl],
                        ob[:, cc * 512: cc * 512 + tl]),
                        deps=[ev], dma="SD")
                tile_store = prev_store
        return k.barrier()

    set_mu("mu_k", a1_done + [t_omm])
    d_k = mlp(kst_h, wt["kW1"], pcols["kb1"], wt["kW2"], pcols["kb2"], (0,), a1_done)
    set_mu("mu_v", d_k)
    d_v = mlp(vst_h, wt["vW1"], pcols["vb1"], wt["vW2"], pcols["vb2"], (0,), d_k)
    set_mu("mu_r", d_v)
    d_r = mlp(r_h, wt["rW1"], pcols["rb1"], wt["rW2"], pcols["rb2"], (W,), d_v)
    a2_done = d_r

    cvt_state = {"tok": None}

    def lowrank(A_t, B_t, bvec, tok_range, deps0, post):
        with nc.sbuf_tensor(uname("lAb"), [128, C], BF16) as lAb, \
             nc.sbuf_tensor(uname("lBb"), [128, C], BF16) as lBb, \
             nc.sbuf_tensor(uname("lld"), [128, C], F32) as lld, \
             nc.sbuf_tensor(uname("lbc"), [128, NCHB], F32) as lbc, \
             nc.sbuf_tensor(uname("lmid"), [128, 512], BF16) as lmid, \
             nc.sbuf_tensor(uname("lout"), [128, 512], F32) as lout, \
             nc.sbuf_tensor(uname("lo2"), [128, 512], F32) as lo2:
            atok = []
            for kc in range(NCHB):
                ld = k.emit("S", lambda e, kc=kc, A_t=A_t: e.dma_start(
                    lld[:, 0:DL], bass.AP(A_t, kc * 128 * DL, [[DL, 128], [1, DL]])),
                    deps=list(deps0) + ([atok[-1]] if atok else []), dma="LD")
                atok.append(k.emit("V", lambda e, kc=kc: e.tensor_copy(
                    lAb[:, kc * DL: kc * DL + DL], lld[:, 0:DL]), deps=[ld]))
            bld = k.emit("S", lambda e, B_t=B_t: e.dma_start(
                lld[:, 0:C], bass.AP(B_t, 0, [[C, 128], [1, C]])),
                deps=[atok[-1]], dma="LD")
            btok = k.emit("V", lambda e: e.tensor_copy(lBb[:, :], lld[:, 0:C]),
                          deps=[bld])
            mmt = None
            for kc in range(NCHB):
                mmt = k.emit("T", lambda e, kc=kc, st=(kc == 0): e.matmul(
                    pb[5][0:DL, 0:1], lAb[:, kc * DL: kc * DL + DL],
                    bf_mu[:, kc: kc + 1], start=st, stop=(kc == NCHB - 1)),
                    deps=[atok[kc], mu_state["tok"]])
            muAev = k.emit("V", lambda e: e.tensor_copy(
                lmid[0:DL, 0:1], pb[5][0:DL, 0:1]), deps=[mmt])
            mmt = None
            for cc in range(NCHB):
                mmt = k.emit("T", lambda e, cc=cc, st=(cc == 0): e.matmul(
                    pb[5][:, 4 + cc: 5 + cc], lBb[:, cc * 128: cc * 128 + 128],
                    lmid[0:DL, 0:1], start=st, stop=(cc == NCHB - 1)),
                    deps=[muAev, btok])
            if bvec is not None:
                bc = k.emit("V", lambda e, bvec=bvec: e.tensor_tensor(
                    lbc[:, :], pb[5][:, 4:4 + NCHB], bvec[:, :], Alu.add), deps=[mmt])
            else:
                bc = k.emit("V", lambda e: e.tensor_copy(
                    lbc[:, :], pb[5][:, 4:4 + NCHB]), deps=[mmt])
            tstart = tok_range[0]
            tiles = [(max(t0, tstart), t0 + tl - max(t0, tstart))
                     for (t0, tl) in A_TILES if t0 + tl > tstart]
            prev_mid = None
            prev_out = None
            for (t0, tl) in tiles:
                mmt = None
                for kc in range(NCHB):
                    mmt = k.emit("T", lambda e, kc=kc, t0=t0, tl=tl, st=(kc == 0):
                                 e.matmul(pb[0][0:DL, 0:tl],
                                          lAb[:, kc * DL: kc * DL + DL],
                                          q_sl(kc, t0, tl),
                                          start=st, stop=(kc == NCHB - 1)),
                                 deps=[atok[kc], prev_mid, a1_done[0]])
                mev = k.emit("V", lambda e, tl=tl: e.tensor_copy(
                    lmid[0:DL, 0:tl], pb[0][0:DL, 0:tl]), deps=[mmt])
                prev_mid = mev
                for cc in range(NCHB):
                    m2 = k.emit("T", lambda e, cc=cc, tl=tl: e.matmul(
                        pb[1][:, 0:tl], lBb[:, cc * 128: cc * 128 + 128],
                        lmid[0:DL, 0:tl], start=True, stop=True),
                        deps=[mev, btok, prev_out])
                    prev_out = post(cc, t0, tl, m2, lbc, lout, lo2)
            res = k.barrier()
        return res

    def post_sig(dst, off):
        def post(cc, t0, tl, m2, lbc, lout, lo2):
            ev = k.emit("A", lambda e, cc=cc, tl=tl, lbc=lbc, lout=lout:
                        e.activation(lout[:, 0:tl], pb[1][:, 0:tl], Act.Sigmoid,
                                     bias=lbc[:, cc: cc + 1], scale=1.0), deps=[m2])
            cv = k.emit("V", lambda e, tl=tl, lout=lout: e.tensor_copy(
                cvt[:, 0:tl], lout[:, 0:tl]), deps=[ev, cvt_state["tok"]])
            st = k.emit("G", lambda e, cc=cc, t0=t0, tl=tl, dst=dst, off=off:
                        e.dma_start(dst[cc][:, t0 - off: t0 - off + tl],
                                    cvt[:, 0:tl]), deps=[cv], dma="SD")
            cvt_state["tok"] = st
            return st
        return post

    set_mu("mu_a", a2_done)
    d_a = lowrank(wt["aA"], wt["aB"], pcols["ab"], (0,), a2_done,
                  post_sig(a_h, 0))
    set_mu("mu_g", d_a)
    d_g = lowrank(wt["gA"], wt["gB"], None, (W,), d_a, post_sig(gate_h, W))

    z_ini = k.emit("V", lambda e: e.memset(zz_s[:, :], 0.0), deps=d_g)
    with nc.sbuf_tensor(uname("ldf"), [128, 512], F32) as ldf, \
         nc.sbuf_tensor(uname("lcf"), [128, 512], F32) as lcf:
        def post_d(cc, t0, tl, m2, lbc, lout, lo2):
            ev0 = k.emit("A", lambda e, cc=cc, tl=tl, lbc=lbc, lo2=lo2:
                         e.activation(lo2[:, 0:tl], pb[1][:, 0:tl], Act.Tanh,
                                      bias=lbc[:, cc: cc + 1], scale=1.0), deps=[m2])
            ev = k.emit("A", lambda e, tl=tl, lo2=lo2, lout=lout: e.activation(
                lout[:, 0:tl], lo2[:, 0:tl], Act.Sigmoid, bias=ec_s[:, 0:1], scale=1.0),
                deps=[ev0])
            sld = k.emit("V", lambda e, tl=tl, lout=lout: e.tensor_scalar(
                ldf[:, 0:tl], lout[:, 0:tl], -DECAY_C, None, Alu.mult),
                deps=[ev, cvt_state["tok"]])
            cb = k.emit("V", lambda e, tl=tl: e.tensor_copy(
                cvt[:, 0:tl], ldf[:, 0:tl]), deps=[sld])
            st1 = k.emit("G", lambda e, cc=cc, t0=t0, tl=tl: e.dma_start(
                ld_h[cc][:, t0: t0 + tl], cvt[:, 0:tl]), deps=[cb], dma="SD")
            sc = None
            for s0 in range(0, tl, L):
                sc = k.emit("V", lambda e, s0=s0: e.tensor_tensor_scan(
                    lcf[:, s0: s0 + L], ldf[:, s0: s0 + L], zz_s[:, :], 0.0,
                    Alu.add, Alu.add), deps=[sld, z_ini])
            st2 = k.emit("G", lambda e, cc=cc, t0=t0, tl=tl: e.dma_start(
                lcl_h[cc][:, t0: t0 + tl], lcf[:, 0:tl]), deps=[sc], dma="SD")
            cvt_state["tok"] = st2
            return st2
        set_mu("mu_d", d_g)
        d_d = lowrank(wt["dA"], wt["dB"], pcols["db"], (0,), d_g, post_d)
    a3_done = d_d
    stackQ.close()

    with nc.sbuf_tensor(uname("t_k"), [128, 512], BF16) as t_k, \
         nc.sbuf_tensor(uname("t_v"), [128, 512], BF16) as t_v, \
         nc.sbuf_tensor(uname("t_a"), [128, 512], BF16) as t_a, \
         nc.sbuf_tensor(uname("t_lc"), [128, 512], F32) as t_lc, \
         nc.sbuf_tensor(uname("t_ld"), [128, 512], BF16) as t_ld, \
         nc.sbuf_tensor(uname("t_e1"), [128, 512], F32) as t_e1, \
         nc.sbuf_tensor(uname("t_e2"), [128, 512], F32) as t_e2, \
         nc.sbuf_tensor(uname("t_e3"), [128, 512], F32) as t_e3, \
         nc.sbuf_tensor(uname("t_cf"), [128, 512], F32) as t_cf, \
         nc.sbuf_tensor(uname("t_rm"), [128, 512], BF16) as t_rm, \
         nc.sbuf_tensor(uname("t_s1"), [128, 512], F32) as t_s1, \
         nc.sbuf_tensor(uname("t_s2"), [128, 512], BF16) as t_s2, \
         nc.sbuf_tensor(uname("t_kn"), [128, 512], BF16) as t_kn, \
         nc.sbuf_tensor(uname("t_o"), [128, 7 * 512], BF16) as t_o:
        prev_st = None
        for (t0, tl) in A_TILES:
            for chb in range(NCHB):
                lk = k.emit("S", lambda e, c=chb, t0=t0, tl=tl: e.dma_start(
                    t_k[:, 0:tl], kst_h[c][:, t0: t0 + tl]),
                    deps=a3_done + [prev_st], dma="LD")
                lv = k.emit("S", lambda e, c=chb, t0=t0, tl=tl: e.dma_start(
                    t_v[:, 0:tl], vst_h[c][:, t0: t0 + tl]), dma="LD")
                la = k.emit("S", lambda e, c=chb, t0=t0, tl=tl: e.dma_start(
                    t_a[:, 0:tl], a_h[c][:, t0: t0 + tl]), dma="LD")
                llc = k.emit("S", lambda e, c=chb, t0=t0, tl=tl: e.dma_start(
                    t_lc[:, 0:tl], lcl_h[c][:, t0: t0 + tl]), dma="LD")
                lld_ = k.emit("S", lambda e, c=chb, t0=t0, tl=tl: e.dma_start(
                    t_ld[:, 0:tl], ld_h[c][:, t0: t0 + tl]), dma="LD")
                rm = k.emit("V", lambda e, c=chb, tl=tl: e.tensor_scalar(
                    t_rm[:, 0:tl], t_k[:, 0:tl], pcols["rem_mult"][:, c: c + 1],
                    None, Alu.mult), deps=[lk])
                rm2 = k.emit("V", lambda e, tl=tl: e.tensor_mul(
                    t_s2[:, 0:tl], t_rm[:, 0:tl], t_rm[:, 0:tl]), deps=[rm])
                ssm = k.emit("T", lambda e, tl=tl: e.matmul(
                    pb[4][:, 0:tl], cs["bones"][:, :], t_s2[:, 0:tl],
                    start=True, stop=True), deps=[rm2, prev_st])
                sq = k.emit("A", lambda e, tl=tl: e.activation(
                    t_s1[:, 0:tl], pb[4][:, 0:tl], Act.Sqrt, bias=ec_s[:, 0:1], scale=1.0),
                    deps=[ssm])
                nm = k.emit("V", lambda e, tl=tl: e.tensor_scalar(
                    t_s1[:, 0:tl], t_s1[:, 0:tl], 1e-12, None, Alu.max), deps=[sq])
                il = k.emit("A", lambda e, tl=tl: e.activation(
                    t_s1[:, 0:tl], t_s1[:, 0:tl], Act.Ln, bias=ec_s[:, 0:1], scale=1.0),
                    deps=[nm])
                iv = k.emit("A", lambda e, tl=tl: e.activation(
                    t_s1[:, 0:tl], t_s1[:, 0:tl], Act.Exp, bias=ec_s[:, 0:1], scale=-1.0),
                    deps=[il])
                kn = k.emit("V", lambda e, tl=tl: e.tensor_mul(
                    t_kn[:, 0:tl], t_rm[:, 0:tl], t_s1[:, 0:tl]), deps=[iv])
                lvm = lv
                if t0 == 0:
                    kn = k.emit("V", lambda e: e.tensor_mul(
                        t_kn[:, 0:W], t_kn[:, 0:W], wm_s[:, :]), deps=[kn])
                    lvm = k.emit("V", lambda e: e.tensor_mul(
                        t_v[:, 0:W], t_v[:, 0:W], wm_s[:, :]), deps=[lv])
                s1 = k.emit("V", lambda e, tl=tl: e.tensor_sub(
                    t_s1[:, 0:tl], t_lc[:, 0:tl], t_ld[:, 0:tl]),
                    deps=[llc, lld_, iv])
                e1t = k.emit("A", lambda e, tl=tl: e.activation(
                    t_e1[:, 0:tl], t_s1[:, 0:tl], Act.Exp, bias=ec_s[:, 0:1], scale=1.0),
                    deps=[s1])
                e2t = k.emit("A", lambda e, tl=tl: e.activation(
                    t_e2[:, 0:tl], t_lc[:, 0:tl], Act.Exp, bias=ec_s[:, 0:1], scale=-1.0),
                    deps=[llc])
                cft = k.emit("A", lambda e, tl=tl: e.activation(
                    t_cf[:, 0:tl], t_lc[:, 0:tl], Act.Exp, bias=ec_s[:, 0:1], scale=1.0),
                    deps=[llc])
                ccps = []
                for s0 in range(0, tl, L):
                    ci = (t0 + s0) // L
                    ccps.append(k.emit("A", lambda e, s0=s0, ci=ci, c=chb:
                                       e.activation(
                        cL_s[:, c * NCH + ci: c * NCH + ci + 1],
                        t_cf[:, s0 + L - 1: s0 + L], Act.Copy,
                        bias=0.0, scale=1.0), deps=[cft]))
                e3t = None
                for si_, s0 in enumerate(range(0, tl, L)):
                    ci = (t0 + s0) // L
                    e3t = k.emit("V", lambda e, s0=s0, ci=ci, c=chb: e.tensor_scalar(
                        t_e3[:, s0: s0 + L], t_e2[:, s0: s0 + L],
                        cL_s[:, c * NCH + ci: c * NCH + ci + 1], None, Alu.mult),
                        deps=[e2t, ccps[si_]])

                def osl(i, tl=512):
                    return t_o[:, i * 512: i * 512 + tl]

                def store(dst, tok, i, c=chb, t0=t0, tl=tl, off=0):
                    return k.emit("G", lambda e, dst=dst, c=c, t0=t0, tl=tl,
                                  off=off, i=i:
                                  e.dma_start(dst[c][:, t0 - off: t0 - off + tl],
                                              osl(i, tl)), deps=[tok], dma="SD")
                x1 = k.emit("V", lambda e, tl=tl: e.tensor_mul(
                    t_s2[:, 0:tl], t_a[:, 0:tl], t_kn[:, 0:tl]), deps=[la, kn])
                x2 = k.emit("V", lambda e, tl=tl: e.tensor_mul(
                    osl(0, tl), t_s2[:, 0:tl], t_e1[:, 0:tl]), deps=[x1, e1t])
                store(Af_h, x2, 0)
                x3 = k.emit("V", lambda e, tl=tl: e.tensor_mul(
                    osl(1, tl), t_kn[:, 0:tl], t_e2[:, 0:tl]),
                    deps=[kn, e2t])
                store(Kc_h, x3, 1)
                x4 = k.emit("V", lambda e, tl=tl: e.tensor_mul(
                    osl(2, tl), t_v[:, 0:tl], t_e2[:, 0:tl]),
                    deps=[lvm, e2t])
                store(Vc_h, x4, 2)
                x5 = k.emit("V", lambda e, tl=tl: e.tensor_mul(
                    osl(3, tl), t_v[:, 0:tl], t_e3[:, 0:tl]),
                    deps=[lvm, e3t])
                store(Wv_h, x5, 3)
                x6 = k.emit("V", lambda e, tl=tl: e.scalar_tensor_tensor(
                    osl(4, tl), t_kn[:, 0:tl], -1.0, t_e3[:, 0:tl],
                    Alu.mult, Alu.mult), deps=[kn, e3t])
                store(Wkn_h, x6, 4)
                x7 = k.emit("V", lambda e, c=chb, tl=tl: e.tensor_scalar(
                    t_s2[:, 0:tl], t_a[:, 0:tl], pcols["iclr_mix"][:, c: c + 1],
                    pcols["one_m_mix"][:, c: c + 1], Alu.mult, Alu.add),
                    deps=[la])
                x8 = k.emit("V", lambda e, tl=tl: e.tensor_mul(
                    osl(5, tl), t_k[:, 0:tl], t_s2[:, 0:tl]), deps=[x7, lk])
                prev_st = store(RK_h, x8, 5)
                if t0 >= W:
                    x9 = k.emit("V", lambda e, tl=tl: e.tensor_copy(
                        osl(6, tl), t_cf[:, 0:tl]), deps=[cft])
                    prev_st = store(cfm_h, x9, 6, off=W)
                elif t0 == 0 and tl > W:
                    x9 = k.emit("V", lambda e, tl=tl: e.tensor_copy(
                        osl(6, tl - W), t_cf[:, W:tl]), deps=[cft])
                    prev_st = k.emit("G", lambda e, c=chb, tl=tl: e.dma_start(
                        cfm_h[c][:, 0:tl - W], osl(6, tl - W)),
                        deps=[x9], dma="SD")
    a4_done = k.barrier()

    with nc.sbuf_tensor(uname("b_r"), [128, 512], BF16) as b_r, \
         nc.sbuf_tensor(uname("b_rk"), [128, 512], BF16) as b_rk, \
         nc.sbuf_tensor(uname("b_v"), [128, 512], BF16) as b_v, \
         nc.sbuf_tensor(uname("b_t"), [128, 512], BF16) as b_t, \
         nc.sbuf_tensor(uname("b_o"), [128, 512], BF16) as b_o:
        prev = None
        for (t0, tl) in P_TILES:
            for chb in range(NCHB):
                lr = k.emit("S", lambda e, c=chb, t0=t0, tl=tl: e.dma_start(
                    b_r[:, 0:tl], r_h[c][:, t0 - W: t0 - W + tl]),
                    deps=a4_done + [prev], dma="LD")
                lrk = k.emit("S", lambda e, c=chb, t0=t0, tl=tl: e.dma_start(
                    b_rk[:, 0:tl], RK_h[c][:, t0: t0 + tl]), dma="LD")
                lv = k.emit("S", lambda e, c=chb, t0=t0, tl=tl: e.dma_start(
                    b_v[:, 0:tl], vst_h[c][:, t0: t0 + tl]), dma="LD")
                m1 = k.emit("V", lambda e, tl=tl: e.tensor_mul(
                    b_t[:, 0:tl], b_r[:, 0:tl], b_rk[:, 0:tl]),
                    deps=[lr, lrk, prev])
                mm = k.emit("T", lambda e, tl=tl: e.matmul(
                    pb[4][:, 0:tl], cs["bones"][:, :], b_t[:, 0:tl],
                    start=True, stop=True), deps=[m1, prev])
                s1 = k.emit("V", lambda e, c=chb, tl=tl: e.tensor_scalar(
                    b_o[:, 0:tl], pb[4][:, 0:tl], 0.2,
                    pcols["bonus_mult"][:, c: c + 1], Alu.mult, Alu.mult),
                    deps=[mm, lv])
                s2 = k.emit("V", lambda e, tl=tl: e.tensor_mul(
                    b_o[:, 0:tl], b_o[:, 0:tl], b_v[:, 0:tl]), deps=[s1])
                prev = k.emit("G", lambda e, c=chb, t0=t0, tl=tl: e.dma_start(
                    bon_h[c][:, t0 - W: t0 - W + tl], b_o[:, 0:tl]),
                    deps=[s2], dma="SD")
    a5_done = k.barrier()

    if STOP_AFTER == "A":
        emit_debug(nc, k, dbg)
        k.realize()
        stack.close()
        return nc

    STRS = [Af_h, Kc_h, Vc_h, Wv_h, Wkn_h, RK_h, r_h]
    NSTR = 7
    GW = 256
    stackR = _ES()

    def sbr(name, shape, dtype):
        return stackR.enter_context(nc.sbuf_tensor(uname(name), shape, dtype))
    rs = sbr("rs", [128, 2 * NSTR * NCHB * GW], BF16)
    z0 = sbr("z0", [64, 2 * 3 * H * GW], BF16)
    S0_st = sbr("S0_st", [64, H * 64], BF16)
    cL0 = sbr("cL0", [64, NCHB * NCH], F32)
    m_X = sbr("m_X", [128, 4 * 6 * 128], BF16)
    m_Y = sbr("m_Y", [128, 4 * 7 * 128], BF16)
    m_MvT = sbr("m_MvT", [128, 4 * 128], BF16)
    m_Prk = sbr("m_Prk", [128, 4 * 128], BF16)
    m_tm = sbr("m_tm", [128, 4 * 320], BF16)
    m_V = sbr("m_V", [128, 128], BF16)
    m_U = sbr("m_U", [128, 128], BF16)
    m_Ufm = sbr("m_Ufm", [64, 192], BF16)
    m_Put = sbr("m_Put", [128, 128], BF16)
    m_Z = sbr("m_Z", [64, 64], BF16)
    m_cd = sbr("m_cd", [64, 64], F32)
    m_mo = sbr("m_mo", [64, H * GW], BF16)
    m_scr = sbr("m_scr", [128, 128], F32)

    R_SETTLE = False

    def dup(eng, fn, deps):
        return k.emit(eng, fn, deps=deps)

    def settle(eng, pb_sl_fn, P, Wd, deps):
        if not R_SETTLE:
            return list(deps)
        if eng == "A":
            t = k.emit("A", lambda e, f=pb_sl_fn, P=P, Wd=Wd: e.activation(
                m_scr[0:P, 0:Wd], f(), Act.Copy, bias=0.0, scale=1.0),
                deps=deps)
        else:
            t = k.emit("V", lambda e, f=pb_sl_fn, P=P, Wd=Wd: e.tensor_copy(
                m_scr[0:P, 0:Wd], f()), deps=deps)
        return [t]

    if DEBUG:
        import sys as _sys
        for _nm, _t in (("rs", rs), ("z0", z0), ("S0_st", S0_st),
                        ("cL0", cL0), ("m_X", m_X), ("m_Y", m_Y),
                        ("m_mo", m_mo), ("m_scr", m_scr)):
            _m = nc.lookup_mloc(_t)
            print(f"RALLOC {_nm}: addr={_m.addr} size={_m.size()}",
                  file=_sys.stderr)

    def rs_base(par, si, chb):
        return ((par * NSTR + si) * NCHB + chb) * GW

    def z0_base(par, zi, h):
        return ((par * 3 + zi) * H + h) * GW

    sz0 = k.emit("V", lambda e: e.memset(S0_st[:, :], 0.0), deps=a5_done)
    clc = k.emit("S", lambda e: e.dma_start(cL0[:, :], cL_s[64:128, :]),
                 deps=a5_done, dma="LD")
    stok = [sz0] * H
    bank_free = {i: None for i in range(8)}
    mo_busy = [None] * H
    mo_wait = []

    def load_group(gi, deps):
        g0, gn = R_GROUPS[gi]
        par = gi % 2
        ctr = "LD" if par == 0 else "LD2"
        toks = []
        for si in range(NSTR):
            if si == 6 and g0 + gn <= NWARM:
                continue
            o0 = g0 * L
            on = gn * L
            if si == 6:
                o0 = max(g0, NWARM) * L
                on = (g0 + gn) * L - o0
            for chb in range(NCHB):
                toks.append(k.emit("S", lambda e, si=si, c=chb, o0=o0, on=on,
                                   g0=g0, par=par: e.dma_start(
                    rs[:, rs_base(par, si, c) + o0 - g0 * L:
                       rs_base(par, si, c) + o0 - g0 * L + on],
                    STRS[si][c][:, o0 - (W if si == 6 else 0):
                                o0 - (W if si == 6 else 0) + on]),
                    deps=deps, dma=ctr))
        for h in range(H):
            chb, half = h // 2, (h % 2) * 64
            toks.append(k.emit("S", lambda e, h=h, chb=chb, half=half, g0=g0,
                               gn=gn, par=par: e.dma_start(
                z0[:, z0_base(par, 0, h): z0_base(par, 0, h) + gn * L],
                Af_h[chb][half:half + 64, g0 * L: g0 * L + gn * L]),
                deps=deps, dma=ctr))
            if g0 + gn > NWARM:
                o0 = max(g0, NWARM) * L
                on = (g0 + gn) * L - o0
                toks.append(k.emit("S", lambda e, h=h, chb=chb, half=half,
                                   g0=g0, o0=o0, on=on, par=par: e.dma_start(
                    z0[:, z0_base(par, 1, h) + o0 - g0 * L:
                       z0_base(par, 1, h) + o0 - g0 * L + on],
                    r_h[chb][half:half + 64, o0 - W: o0 - W + on]),
                    deps=deps, dma=ctr))
                toks.append(k.emit("S", lambda e, h=h, chb=chb, half=half,
                                   g0=g0, o0=o0, on=on, par=par: e.dma_start(
                    z0[:, z0_base(par, 2, h) + o0 - g0 * L:
                       z0_base(par, 2, h) + o0 - g0 * L + on],
                    cfm_h[chb][half:half + 64, o0 - W: o0 - W + on]),
                    deps=deps, dma=ctr))
        return toks

    gtoks = [load_group(0, a5_done)]
    r1_hit = False
    for gi, (g0, gn) in enumerate(R_GROUPS):
        if r1_hit:
            break
        par = gi % 2
        pref_deps = k.engtok("T", "V") + mo_wait
        gl = gtoks[gi][-1]
        mo_wait = []
        for c in range(g0, g0 + gn):
            if r1_hit:
                break
            if c == g0 + 1 and gi + 1 < len(R_GROUPS):
                gtoks.append(load_group(gi + 1, pref_deps))
            co = (c - g0) * L
            warm = c < NWARM
            idn0 = cs["idn64"][0:64, :]

            def MX(hp, j):
                return m_X[:, (hp * 6 + j) * 128:(hp * 6 + j + 1) * 128]

            def MYs(hp, j):
                return m_Y[:, (hp * 7 + j) * 128:(hp * 7 + j + 1) * 128]

            def emit_grams(h, co=co, warm=warm, gl=gl, par=par):
                hp = h % 4
                chb, half = h // 2, (h % 2) * 64

                def fsl(si):
                    b = rs_base(par, si, chb) + co
                    return rs[half:half + 64, b: b + L]
                Af_, Kc_, Vc_ = fsl(0), fsl(1), fsl(2)
                RK_, R_ = fsl(5), fsl(6)
                idn = cs["idn64"][half:half + 64, :]
                k.emit("T", lambda e, Af_=Af_, Kc_=Kc_: e.matmul(
                    pb[0][:, 0:128], Af_, Kc_, start=True, stop=False),
                    deps=[gl, bank_free[0]])
                k.emit("T", lambda e, Af_=Af_, Kc_=Kc_: e.matmul(
                    pb[0][:, 128:256], Kc_, Af_, start=False, stop=False))
                glast = k.emit("T", lambda e, Af_=Af_, Vc_=Vc_: e.matmul(
                    pb[0][:, 256:384], Vc_, Af_, start=False, stop=warm))
                if not warm:
                    glast = k.emit("T", lambda e, RK_=RK_, R_=R_: e.matmul(
                        pb[0][:, 384:512], RK_, R_, start=False, stop=True))
                evX = k.emit("V", lambda e, hp=hp: e.tensor_tensor(
                    MX(hp, 0), pb[0][:, 0:128], cs["mSLn"][:, :], Alu.mult),
                    deps=[glast])
                evY = k.emit("V", lambda e, hp=hp: e.tensor_tensor(
                    MYs(hp, 0), pb[0][:, 128:256], cs["mSUn"][:, :], Alu.mult),
                    deps=[glast])
                evM = k.emit("V", lambda e, hp=hp: e.tensor_tensor(
                    m_MvT[:, hp * 128:hp * 128 + 128], pb[0][:, 256:384],
                    cs["mSU"][:, :], Alu.mult), deps=[glast])
                evP = None
                if not warm:
                    evP = k.emit("V", lambda e, hp=hp: e.tensor_tensor(
                        m_Prk[:, hp * 128:hp * 128 + 128], pb[0][:, 384:512],
                        cs["mUI"][:, :], Alu.mult), deps=[glast])
                bank_free[0] = evP or evM

                tsrc = [fsl(5), fsl(3), fsl(4)] + ([] if warm else
                                                   [fsl(2), fsl(1)])
                tmm = None
                for i, src in enumerate(tsrc):
                    tmm = k.emit("T", lambda e, i=i, src=src, idn=idn,
                                 st=(i == 0), sp=(i == len(tsrc) - 1): e.matmul(
                        pb[1][:, i * 64:(i + 1) * 64], src, idn,
                        start=st, stop=sp),
                        deps=[gl, bank_free[1]] if i == 0 else [])
                evT = k.emit("A", lambda e, hp=hp, n=len(tsrc): e.activation(
                    m_tm[:, hp * 320: hp * 320 + n * 64], pb[1][:, 0:n * 64],
                    Act.Copy, bias=0.0, scale=1.0), deps=[tmm])
                bank_free[1] = evT
                return dict(evX=evX, evY=evY, evM=evM, evP=evP, evT=evT)

            Gs = [emit_grams(0), emit_grams(1)] if not r1_hit else None
            for h0 in range(0, H, 2):
                if r1_hit:
                    break
                Gpair = Gs
                if h0 + 2 < H:
                    Gs = [emit_grams(h0 + 2), emit_grams(h0 + 3)]

                def TM(i, hp):
                    return m_tm[:, hp * 320 + i * 64: hp * 320 + (i + 1) * 64]

                vstate = []
                for hs in (0, 1):
                    h = h0 + hs
                    hp = h % 4
                    Af0_ = z0[:, z0_base(par, 0, h) + co:
                              z0_base(par, 0, h) + co + L]
                    Ssl = S0_st[:, h * 64: h * 64 + 64]
                    G = Gpair[hs]
                    b1 = k.emit("T", lambda e, Af0_=Af0_, Ssl=Ssl, hs=hs:
                                e.matmul(
                        pb[3][:, hs * 64: hs * 64 + 64], Af0_, Ssl,
                        start=(hs == 0), stop=False),
                        deps=[gl, stok[h], bank_free[3]] if hs == 0
                        else [stok[h]])
                    vti = k.emit("T", lambda e, hp=hp, hs=hs, t0s=TM(0, hp):
                                 e.matmul(
                        pb[3][:, hs * 64: hs * 64 + 64],
                        m_MvT[:, hp * 128:hp * 128 + 128], t0s,
                        start=False, stop=False), deps=[G["evM"], G["evT"]])
                    vstate.append(dict(b1=b1, xtok={0: G["evX"]},
                                       ytok={0: G["evY"]}))
                vtok = vti
                for j in range(7):
                    evV = k.emit("A", lambda e: e.activation(
                        m_V[:, :], pb[3][:, 0:128], Act.Copy, bias=0.0,
                        scale=1.0), deps=[vtok])
                    for hs in (0, 1):
                        h = h0 + hs
                        hp = h % 4
                        vs = vstate[hs]
                        vtok = k.emit("T", lambda e, j=j, hp=hp, hs=hs:
                                      e.matmul(
                            pb[3][:, hs * 64: hs * 64 + 64], MYs(hp, j),
                            m_V[:, hs * 64: hs * 64 + 64],
                            start=False, stop=(j == 6 and hs == 1)),
                            deps=[evV, vs["ytok"][j]])
                    if j < 6:
                        for hs in (0, 1):
                            h = h0 + hs
                            hp = h % 4
                            vs = vstate[hs]
                            bk = 4 + hs
                            mmy = k.emit("T", lambda e, j=j, bk=bk, hp=hp:
                                         e.matmul(
                                pb[bk][:, 0:128], MX(hp, j), MYs(hp, j),
                                start=True, stop=(j == 5)),
                                deps=[vs["xtok"][j], vs["ytok"][j],
                                      bank_free[bk]])
                            mmx = None
                            if j < 5:
                                mmx = k.emit("T", lambda e, j=j, bk=bk, hp=hp:
                                             e.matmul(
                                    pb[bk][:, 128:256], MYs(hp, j), MX(hp, j),
                                    start=False, stop=True))
                            ee = "A" if hs == 0 else "V"
                            if ee == "A":
                                evy = k.emit("A", lambda e, j=j, bk=bk, hp=hp:
                                             e.activation(
                                    MYs(hp, j + 1), pb[bk][:, 0:128],
                                    Act.Copy, bias=0.0, scale=1.0),
                                    deps=[mmx or mmy])
                            else:
                                evy = k.emit("V", lambda e, j=j, bk=bk, hp=hp:
                                             e.tensor_copy(
                                    MYs(hp, j + 1), pb[bk][:, 0:128]),
                                    deps=[mmx or mmy])
                            vs["ytok"][j + 1] = evy
                            if j < 5:
                                if ee == "A":
                                    evx = k.emit("A", lambda e, j=j, bk=bk,
                                                 hp=hp: e.activation(
                                        MX(hp, j + 1), pb[bk][:, 128:256],
                                        Act.Copy, bias=0.0, scale=1.0),
                                        deps=[mmx])
                                else:
                                    evx = k.emit("V", lambda e, j=j, bk=bk,
                                                 hp=hp: e.tensor_copy(
                                        MX(hp, j + 1), pb[bk][:, 128:256]),
                                        deps=[mmx])
                                vs["xtok"][j + 1] = evx
                                bank_free[bk] = evx
                            else:
                                bank_free[bk] = evy
                evU = k.emit("A", lambda e: e.activation(
                    m_U[:, :], pb[3][:, 0:128], Act.Copy, bias=0.0, scale=1.0),
                    deps=[vtok])
                bank_free[3] = evU

                for hs in (0, 1):
                    h = h0 + hs
                    hp = h % 4
                    chb, half = h // 2, (h % 2) * 64
                    warmh = warm
                    G = Gpair[hs]
                    evP, evT = G["evP"], G["evT"]
                    r0_ = z0[:, z0_base(par, 1, h) + co:
                             z0_base(par, 1, h) + co + L]
                    cf0_ = z0[:, z0_base(par, 2, h) + co:
                              z0_base(par, 2, h) + co + L]
                    Ssl = S0_st[:, h * 64: h * 64 + 64]
                    if half == 0:
                        cLc = cL_s[0:64, chb * NCH + c: chb * NCH + c + 1]
                    else:
                        cLc = cL0[:, chb * NCH + c: chb * NCH + c + 1]
                    z1 = None
                    if not warmh:
                        z1 = k.emit("T", lambda e, Ssl=Ssl, idn0=idn0:
                                    e.matmul(
                            pb[2][0:64, 0:64], Ssl, idn0,
                            start=True, stop=False),
                            deps=[stok[h], bank_free[2]])
                        z2 = k.emit("T", lambda e, hs=hs: e.matmul(
                            pb[2][0:64, 64:192],
                            m_U[:, hs * 64: hs * 64 + 64],
                            cs["idn128"][:, :],
                            start=False, stop=True), deps=[evU])
                        evUf = k.emit("A", lambda e: e.activation(
                            m_Ufm[:, :], pb[2][0:64, 0:192], Act.Copy,
                            bias=0.0, scale=1.0), deps=[z2])
                        evZ = evUf
                        z3 = k.emit("T", lambda e, r0_=r0_: e.matmul(
                            pb[2][:, 192:320], m_Ufm[:, 64:192], r0_,
                            start=True, stop=True), deps=[evUf, gl])
                        evPu = k.emit("V", lambda e: e.tensor_tensor(
                            m_Put[:, :], pb[2][:, 192:320], cs["mUIn"][:, :],
                            Alu.mult), deps=[z3])
                        bank_free[2] = evPu
                        o1 = k.emit("T", lambda e, r0_=r0_: e.matmul(
                            pb[6][0:64, 0:128], m_Ufm[:, 0:64], r0_,
                            start=True,
                            stop=False), deps=[evZ, bank_free[6]])
                        o2 = k.emit("T", lambda e, hp=hp, t3s=TM(3, hp):
                                    e.matmul(
                            pb[6][0:64, 0:128], t3s,
                            m_Prk[:, hp * 128:hp * 128 + 128], start=False,
                            stop=False), deps=[evP, evT])
                        o3 = k.emit("T", lambda e, t4s=TM(4, hp): e.matmul(
                            pb[6][0:64, 0:128], t4s, m_Put[:, :], start=False,
                            stop=True), deps=[evPu])
                        mo_dst = m_mo[:, h * GW + co: h * GW + co + L]
                        evO = k.emit("V", lambda e, mo_dst=mo_dst, cf0_=cf0_:
                                     e.tensor_tensor(mo_dst,
                                                     pb[6][0:64, 0:128],
                                                     cf0_, Alu.mult),
                                     deps=[o3, mo_busy[h], gl])
                        bank_free[6] = evO
                        mo_busy[h] = evO
                    s1 = k.emit("T", lambda e, t1s=TM(1, hp), t0s=TM(0, hp):
                                e.matmul(
                        pb[7][0:64, 0:64], t1s, t0s, start=True, stop=False),
                        deps=[evT, bank_free[7]])
                    s2 = k.emit("T", lambda e, hs=hs, t2s=TM(2, hp): e.matmul(
                        pb[7][0:64, 0:64], t2s,
                        m_U[:, hs * 64: hs * 64 + 64],
                        start=False, stop=True), deps=[evU])
                    cd = k.emit("V", lambda e, Ssl=Ssl, cLc=cLc:
                                e.tensor_scalar(
                        m_cd[:, :], Ssl, cLc, None, Alu.mult),
                        deps=[stok[h], s1, clc])
                    sn = k.emit("V", lambda e, Ssl=Ssl: e.tensor_tensor(
                        Ssl, pb[7][0:64, 0:64], m_cd[:, :], Alu.add),
                        deps=[s2, cd, vstate[hs]["b1"], z1])
                    stok[h] = sn
                    bank_free[7] = sn
                if STOP_AFTER == "R1" and c == R1_C and h == R1_H:
                    r1_hit = True
                    dsc = sbr("dsc", [128, 1024], F32)
                    allb = k.barrier()
                    pd = [None]

                    def rdump(nm, sl, P, Wd):
                        t = nc.dram_tensor("dR_" + nm, [P, Wd], F32,
                                           kind="ExternalOutput")
                        cv = k.emit("V", lambda e, sl=sl, P=P, Wd=Wd:
                                    e.tensor_copy(dsc[0:P, 0:Wd], sl),
                                    deps=allb + [pd[0]])
                        pd[0] = k.emit("G", lambda e, t=t, P=P, Wd=Wd:
                                       e.dma_start(t[:, :], dsc[0:P, 0:Wd]),
                                       deps=[cv], dma="SD")

                    for si, nm in enumerate(["Af", "Kc", "Vc", "Wv", "Wkn",
                                             "RK", "R"]):
                        b = rs_base(par, si, chb) + co
                        rdump(nm, rs[half:half + 64, b: b + L], 64, L)
                    rdump("Af0", z0[:, z0_base(par, 0, h) + co:
                                    z0_base(par, 0, h) + co + L], 64, L)
                    rdump("r0", z0[:, z0_base(par, 1, h) + co:
                                   z0_base(par, 1, h) + co + L], 64, L)
                    rdump("cf0", z0[:, z0_base(par, 2, h) + co:
                                    z0_base(par, 2, h) + co + L], 64, L)
                    rdump("X", m_X[:, hp * 768: hp * 768 + 768],
                          128, 6 * 128)
                    rdump("Y", m_Y[:, hp * 896: hp * 896 + 896],
                          128, 7 * 128)
                    rdump("MvT", m_MvT[:, hp * 128: hp * 128 + 128], 128, 128)
                    rdump("Prk", m_Prk[:, hp * 128: hp * 128 + 128], 128, 128)
                    rdump("tm", m_tm[:, hp * 320: hp * 320 + 320], 128, 320)
                    rdump("U", m_U[:, :], 128, 64)
                    rdump("Ufm", m_Ufm[:, :], 64, 128)
                    rdump("Z", m_Z[:, :], 64, 64)
                    rdump("Put", m_Put[:, :], 128, 128)
                    rdump("cd", m_cd[:, :], 64, 64)
                    rdump("Sn", S0_st[:, h * 64: h * 64 + 64], 64, 64)
                    rdump("mo", m_mo[:, h * GW + co: h * GW + co + L], 64, L)
                    rdump("cLs", cL_s[:, :], 128, NCHB * NCH)
                    rdump("cL0", cL0[:, :], 64, NCHB * NCH)
                    break
        if g0 + gn > NWARM and not r1_hit:
            o0 = max(g0, NWARM) * L
            on = (g0 + gn) * L - o0
            for h in range(H):
                chb, half = h // 2, (h % 2) * 64
                mo_busy[h] = k.emit("G", lambda e, h=h, chb=chb, half=half,
                                    o0=o0, on=on, g0=g0: e.dma_start(
                    mo_h[chb][half:half + 64, o0 - W: o0 - W + on],
                    m_mo[:, h * GW + o0 - g0 * L: h * GW + o0 - g0 * L + on]),
                    deps=[mo_busy[h]], dma="SD")
                mo_wait.append(mo_busy[h])
    r_done = k.barrier()
    stackR.close()

    if STOP_AFTER in ("R", "R1"):
        if STOP_AFTER == "R":
            emit_debug(nc, k, dbg)
        k.realize()
        stack.close()
        return nc

    build_phase_p(nc, k, pb, cs, pcols, sc_s, ec_s, wt, mo_h, bon_h, gate_h,
                  xT, out_t, r_done, sb)

    emit_debug(nc, k, dbg)
    k.realize()
    stack.close()
    return nc


def emit_debug(nc, k, dbg):
    if not DEBUG:
        return
    with nc.sbuf_tensor(uname("dbf"), [128, TT], F32) as dbf, \
         nc.sbuf_tensor(uname("dbb"), [128, TT], BF16) as dbb:
        import sys as _sys
        for _nm, _t in (("dbf", dbf), ("dbb", dbb)):
            _m = nc.lookup_mloc(_t)
            print(f"DALLOC {_nm}: addr={_m.addr} size={_m.size()}",
                  file=_sys.stderr)
        dd = k.barrier()
        prev = None
        for nm, (ot, st) in dbg.items():
            wdt = st[0].shape[1]
            isf = st[0].dtype == F32
            for chb in range(NCHB):
                l1 = k.emit("S", lambda e, st=st, c=chb, wdt=wdt, isf=isf:
                            e.dma_start(dbf[:, 0:wdt] if isf else dbb[:, 0:wdt],
                                        st[c][:, :]),
                            deps=dd + [prev], dma="LD")
                cv = k.emit("V", lambda e, wdt=wdt, isf=isf: e.tensor_copy(
                    dbf[:, 0:wdt], dbf[:, 0:wdt] if isf else dbb[:, 0:wdt]),
                    deps=[l1])
                prev = k.emit("G", lambda e, ot=ot, c=chb, wdt=wdt:
                              e.dma_start(bass.AP(ot, c * 128 * wdt,
                                                  [[wdt, 128], [1, wdt]]),
                                          dbf[:, 0:wdt]), deps=[cv], dma="SD")


def build_phase_p(nc, k, pb, cs, pcols, sc_s, ec_s, wt, mo_h, bon_h, gate_h,
                  xT, out_t, r_done, sb):
    invs = sb("invs", [128, 1], F32)
    scol = sb("scol", [128, 1], F32)
    sb16 = sb("sb16", [1, 1], BF16)
    t0_ = k.emit("V", lambda e: e.tensor_scalar(
        sc_s[:, :], sc_s[:, :], 1e-5, None, Alu.max), deps=r_done)
    t1_ = k.emit("V", lambda e: e.tensor_copy(sb16[:, :], sc_s[:, :]), deps=[t0_])
    t1_ = k.emit("V", lambda e: e.tensor_copy(sb16[:, :], sc_s[:, :]), deps=[t1_])
    t2_ = k.emit("T", lambda e: e.matmul(
        pb[5][:, 0:1], cs["ones128"][0:1, :], sb16[:, :], start=True, stop=True),
        deps=[t1_] + r_done)
    t3_ = k.emit("V", lambda e: e.tensor_copy(scol[:, :], pb[5][:, 0:1]),
                 deps=[t2_])
    t3_ = k.emit("V", lambda e: e.tensor_copy(scol[:, :], pb[5][:, 0:1]),
                 deps=[t3_])
    t4_ = k.emit("V", lambda e: e.reciprocal(invs[:, :], scol[:, :]), deps=[t3_])
    t4_ = k.emit("V", lambda e: e.reciprocal(invs[:, :], scol[:, :]), deps=[t4_])

    ow1b = sb("ow1b", [128, NCHB * EP], BF16)
    ow2b = sb("ow2b", [128, NEB * C], BF16)
    fwb = sb("fwb", [128, NCHB * C], BF16)
    ob1c = sb("ob1c", [128, NEB], F32)
    with nc.sbuf_tensor(uname("pwld"), [128, EP], F32) as pwld:
        ltok = None
        wt1 = []
        for kc in range(NCHB):
            ld = k.emit("S", lambda e, kc=kc: e.dma_start(
                pwld[:, 0:EP], bass.AP(wt["oW1"], kc * 128 * EP,
                                       [[EP, 128], [1, EP]])),
                deps=r_done + [ltok], dma="LD")
            ltok = k.emit("V", lambda e, kc=kc: e.tensor_copy(
                ow1b[:, kc * EP: kc * EP + EP], pwld[:, 0:EP]), deps=[ld])
            wt1.append(ltok)
        for ec in range(NEB):
            ld = k.emit("S", lambda e, ec=ec: e.dma_start(
                pwld[:, 0:C], bass.AP(wt["oW2"], ec * 128 * C, [[C, 128], [1, C]])),
                deps=[ltok], dma="LD")
            ltok = k.emit("V", lambda e, ec=ec: e.tensor_copy(
                ow2b[:, ec * C: ec * C + C], pwld[:, 0:C]), deps=[ld])
            wt1.append(ltok)
        for kc in range(NCHB):
            ld = k.emit("S", lambda e, kc=kc: e.dma_start(
                pwld[:, 0:C], bass.AP(wt["fWT"], kc * 128 * C, [[C, 128], [1, C]])),
                deps=[ltok], dma="LD")
            ltok = k.emit("V", lambda e, kc=kc: e.tensor_scalar(
                fwb[:, kc * C: kc * C + C], pwld[:, 0:C], -1.0, None, Alu.add),
                deps=[ld])
            wt1.append(ltok)
        cb = k.emit("V", lambda e: e.tensor_copy(ob1c[:, :], pcols["ob1"][:, :]),
                    deps=r_done)
        wdone = ltok

    sq_b = sb("p_sq", [128, NCHB * 512], BF16)
    ot_b = sb("p_ot", [128, NCHB * 512], BF16)
    o2_b = sb("p_o2", [128, NCHB * 512], BF16)
    hb_b = sb("p_hb", [128, NEB * 512], BF16)
    x2_b = sb("p_x2", [128, NCHB * 512], F32)
    y_b = sb("p_y", [128, NCHB * 512], BF16)
    stm = sb("p_stm", [128, 512], F32)
    strr = sb("p_str", [128, 512], F32)
    tq1 = sb("p_tq1", [128, 512], F32)
    tq2 = sb("p_tq2", [128, 512], F32)
    tq3 = sb("p_tq3", [128, 512], F32)
    tq4 = sb("p_tq4", [128, 512], F32)
    tqi = sb("p_tqi", [128, 512], I32)

    if DEBUG:
        import sys as _sys
        for _nm, _t in (("sq_b", sq_b), ("ot_b", ot_b), ("o2_b", o2_b),
                        ("hb_b", hb_b), ("x2_b", x2_b), ("y_b", y_b),
                        ("stm", stm), ("strr", strr), ("tq1", tq1),
                        ("tq2", tq2), ("tq3", tq3), ("tq4", tq4),
                        ("tqi", tqi), ("ow1b", ow1b), ("ow2b", ow2b),
                        ("fwb", fwb), ("invs", invs), ("scol", scol)):
            _m = nc.lookup_mloc(_t)
            print(f"PALLOC {_nm}: addr={_m.addr} size={_m.size()}",
                  file=_sys.stderr)

    psfree = {i: None for i in range(8)}

    def ln_stats(src_fn, eps, deps, tl):
        sqs = []
        for chb in range(NCHB):
            sqs.append(k.emit("V", lambda e, c=chb, tl=tl: e.tensor_mul(
                sq_b[:, c * 512: c * 512 + tl], src_fn(c), src_fn(c)),
                deps=list(deps)))
        mm1 = mm2 = None
        for chb in range(NCHB):
            mm1 = k.emit("T", lambda e, c=chb, tl=tl: e.matmul(
                pb[6][:, 0:tl], cs["ones128"][:, :], src_fn(c),
                start=(c == 0), stop=(c == NCHB - 1)),
                deps=list(deps) + [psfree[6]])
        for chb in range(NCHB):
            mm2 = k.emit("T", lambda e, c=chb, tl=tl: e.matmul(
                pb[7][:, 0:tl], cs["ones128"][:, :],
                sq_b[:, c * 512: c * 512 + tl],
                start=(c == 0), stop=(c == NCHB - 1)),
                deps=[sqs[chb], psfree[7]])
        e1 = k.emit("V", lambda e, tl=tl: e.tensor_scalar(
            stm[:, 0:tl], pb[6][:, 0:tl], 1.0 / C, None, Alu.mult), deps=[mm1])
        e2 = k.emit("V", lambda e, tl=tl: e.tensor_mul(
            strr[:, 0:tl], stm[:, 0:tl], stm[:, 0:tl]), deps=[e1])
        e3 = k.emit("V", lambda e, tl=tl: e.scalar_tensor_tensor(
            strr[:, 0:tl], pb[7][:, 0:tl], 1.0 / C, strr[:, 0:tl],
            Alu.mult, Alu.subtract), deps=[mm2, e2])
        e4 = k.emit("A", lambda e, tl=tl, eps=eps: e.activation(
            strr[:, 0:tl], strr[:, 0:tl], Act.Ln, bias=(ec_s[:, 1:2] if eps == 1e-5 else ec_s[:, 2:3]), scale=1.0), deps=[e3])
        e5 = k.emit("A", lambda e, tl=tl: e.activation(
            strr[:, 0:tl], strr[:, 0:tl], Act.Exp, bias=ec_s[:, 0:1], scale=-0.5),
            deps=[e4])
        psfree[6] = e1
        psfree[7] = e3
        return e5

    pdbg = {}
    if DEBUG:
        for nm, dt_ in (("pg", BF16), ("pm", BF16), ("px", F32),
                        ("pq", BF16), ("pf", BF16), ("pa", BF16)):
            pdbg[nm] = nc.dram_tensor("dbg_" + nm, [C, TC], dt_,
                                      kind="ExternalOutput")
        for nm in ("ps", "pr"):
            pdbg[nm] = nc.dram_tensor("dbg_" + nm, [128, TC], F32,
                                      kind="ExternalOutput")
        pdbg["pv"] = nc.dram_tensor("dbg_pv", [128, 4], F32,
                                    kind="ExternalOutput")
        pdbg["pz"] = nc.dram_tensor("dbg_pz", [128, 512], BF16,
                                    kind="ExternalOutput")
        k.emit("G", lambda e: e.dma_start(
            pdbg["pv"][:, 0:1], scol[:, 0:1],
            allow_slow_non_contiguous=True), deps=[t4_], dma="SD")
        k.emit("G", lambda e: e.dma_start(
            pdbg["pv"][:, 1:2], invs[:, 0:1],
            allow_slow_non_contiguous=True), deps=[t4_], dma="SD")

    def pdump(nm, buf, chb, t0, tl, dep):
        if not DEBUG:
            return dep
        return k.emit("G", lambda e, nm=nm, c=chb, t0=t0, tl=tl, buf=buf:
                      e.dma_start(bass.AP(pdbg[nm], c * 128 * TC + (t0 - W),
                                          [[TC, 128], [1, tl]]),
                                  buf[:, c * 512: c * 512 + tl]),
                      deps=[dep], dma="SD")

    prev_tile = None
    for (t0, tl) in P_TILES:
        base = [wdone, t4_, cb, prev_tile]
        lts, gts = [], []
        for chb in range(NCHB):
            l1 = k.emit("S", lambda e, c=chb, t0=t0, tl=tl: e.dma_start(
                ot_b[:, c * 512: c * 512 + tl], mo_h[c][:, t0 - W: t0 - W + tl]),
                deps=base, dma="LD")
            l1 = k.emit("S", lambda e, c=chb, t0=t0, tl=tl: e.dma_start(
                ot_b[:, c * 512: c * 512 + tl], mo_h[c][:, t0 - W: t0 - W + tl]),
                deps=[l1], dma="LD")
            l2 = k.emit("S", lambda e, c=chb, t0=t0, tl=tl: e.dma_start(
                sq_b[:, c * 512: c * 512 + tl], bon_h[c][:, t0 - W: t0 - W + tl]),
                deps=base, dma="LD")
            l2 = k.emit("S", lambda e, c=chb, t0=t0, tl=tl: e.dma_start(
                sq_b[:, c * 512: c * 512 + tl], bon_h[c][:, t0 - W: t0 - W + tl]),
                deps=[l2], dma="LD")
            lts.append(pdump("pa", ot_b, chb, t0, tl, k.emit(
                "V", lambda e, c=chb, tl=tl: e.tensor_add(
                    ot_b[:, c * 512: c * 512 + tl],
                    ot_b[:, c * 512: c * 512 + tl],
                    sq_b[:, c * 512: c * 512 + tl]),
                deps=[l1, l2, prev_tile])))
            g1_ = k.emit("S", lambda e, c=chb, t0=t0, tl=tl: e.dma_start(
                o2_b[:, c * 512: c * 512 + tl],
                gate_h[c][:, t0 - W: t0 - W + tl]), deps=base, dma="LD")
            gts.append(k.emit("S", lambda e, c=chb, t0=t0, tl=tl: e.dma_start(
                o2_b[:, c * 512: c * 512 + tl],
                gate_h[c][:, t0 - W: t0 - W + tl]), deps=[g1_], dma="LD"))
        r5 = ln_stats(lambda c: ot_b[:, c * 512: c * 512 + tl], 1e-6, lts, tl)
        if DEBUG and t0 == P_TILES[0][0]:
            with nc.sbuf_tensor(uname("diagz"), [128, 512], BF16) as diag_b:
                zl = k.emit("S", lambda e: e.dma_start(
                    diag_b[:, :], mo_h[4][:, 0:512]), deps=[r5], dma="LD")
                r5 = k.emit("G", lambda e: e.dma_start(
                    pdbg["pz"][:, :], diag_b[:, :]), deps=[zl], dma="SD")
        if DEBUG:
            r5 = k.emit("G", lambda e, t0=t0, tl=tl: e.dma_start(
                pdbg["pr"][:, t0 - W: t0 - W + tl], strr[:, 0:tl]),
                deps=[r5], dma="SD")
            r5 = k.emit("G", lambda e, t0=t0, tl=tl: e.dma_start(
                pdbg["ps"][:, t0 - W: t0 - W + tl], stm[:, 0:tl]),
                deps=[r5], dma="SD")
        o2t = []
        for chb in range(NCHB):
            s1 = k.emit("V", lambda e, c=chb, tl=tl: e.tensor_sub(
                ot_b[:, c * 512: c * 512 + tl], ot_b[:, c * 512: c * 512 + tl],
                stm[:, 0:tl]), deps=[r5])
            s2 = k.emit("V", lambda e, c=chb, tl=tl: e.tensor_mul(
                ot_b[:, c * 512: c * 512 + tl], ot_b[:, c * 512: c * 512 + tl],
                strr[:, 0:tl]), deps=[s1])
            s3 = k.emit("V", lambda e, c=chb, tl=tl: e.tensor_scalar(
                ot_b[:, c * 512: c * 512 + tl], ot_b[:, c * 512: c * 512 + tl],
                pcols["ln_g"][:, c: c + 1], pcols["ln_b"][:, c: c + 1],
                Alu.mult, Alu.add), deps=[s2])
            o2t.append(pdump("pg", o2_b, chb, t0, tl, k.emit(
                "V", lambda e, c=chb, tl=tl: e.tensor_mul(
                    o2_b[:, c * 512: c * 512 + tl],
                    o2_b[:, c * 512: c * 512 + tl],
                    ot_b[:, c * 512: c * 512 + tl]), deps=[s3, gts[chb]])))
        htok = [None] * NEB
        for ec in range(NEB):
            bk = ec % 2
            mmt = None
            for kc in range(NCHB):
                mmt = k.emit("T", lambda e, kc=kc, ec=ec, tl=tl, bk=bk,
                             st=(kc == 0): e.matmul(
                    pb[bk][:, 0:tl],
                    ow1b[:, kc * EP + ec * 128: kc * EP + ec * 128 + 128],
                    o2_b[:, kc * 512: kc * 512 + tl],
                    start=st, stop=(kc == NCHB - 1)),
                    deps=[o2t[kc], psfree.get(bk)])
            htok[ec] = k.emit("A", lambda e, ec=ec, tl=tl, bk=bk: e.activation(
                hb_b[:, ec * 512: ec * 512 + tl], pb[bk][:, 0:tl], Act.Gelu,
                bias=ob1c[:, ec: ec + 1], scale=1.0), deps=[mmt])
            psfree[bk] = htok[ec]
        ootok = []
        for cc in range(NCHB):
            bk = 2 + cc % 2
            mmt = None
            for ec in range(NEB):
                mmt = k.emit("T", lambda e, ec=ec, cc=cc, tl=tl, bk=bk,
                             st=(ec == 0): e.matmul(
                    pb[bk][:, 0:tl],
                    ow2b[:, ec * C + cc * 128: ec * C + cc * 128 + 128],
                    hb_b[:, ec * 512: ec * 512 + tl],
                    start=st, stop=(ec == NEB - 1)),
                    deps=[htok[ec], psfree.get(bk)])
            ev = k.emit("V", lambda e, cc=cc, tl=tl, bk=bk: e.tensor_scalar(
                ot_b[:, cc * 512: cc * 512 + tl], pb[bk][:, 0:tl],
                pcols["ob2"][:, cc: cc + 1], None, Alu.add), deps=[mmt])
            psfree[bk] = ev
            ootok.append(pdump("pm", ot_b, cc, t0, tl, ev))
        r5 = ln_stats(lambda c: ot_b[:, c * 512: c * 512 + tl], 1e-5, ootok, tl)
        x2t = []
        for chb in range(NCHB):
            lx = k.emit("S", lambda e, c=chb, t0=t0, tl=tl: e.dma_start(
                x2_b[:, c * 512: c * 512 + tl],
                bass.AP(xT, c * 128 * TT + t0, [[TT, 128], [1, tl]])),
                deps=base, dma="LD")
            lx = k.emit("S", lambda e, c=chb, t0=t0, tl=tl: e.dma_start(
                x2_b[:, c * 512: c * 512 + tl],
                bass.AP(xT, c * 128 * TT + t0, [[TT, 128], [1, tl]])),
                deps=[lx], dma="LD")
            s1 = k.emit("V", lambda e, c=chb, tl=tl: e.tensor_sub(
                tq1[:, 0:tl], ot_b[:, c * 512: c * 512 + tl], stm[:, 0:tl]),
                deps=[r5, lx])
            s2 = k.emit("V", lambda e, tl=tl: e.tensor_mul(
                tq1[:, 0:tl], tq1[:, 0:tl], strr[:, 0:tl]), deps=[s1])
            s3 = k.emit("V", lambda e, c=chb, tl=tl: e.tensor_add(
                x2_b[:, c * 512: c * 512 + tl], x2_b[:, c * 512: c * 512 + tl],
                tq1[:, 0:tl]), deps=[s2])
            x2t.append(pdump("px", x2_b, chb, t0, tl, k.emit(
                "V", lambda e, c=chb, tl=tl: e.tensor_copy(
                    y_b[:, c * 512: c * 512 + tl],
                    x2_b[:, c * 512: c * 512 + tl]), deps=[s3])))
        r5 = ln_stats(lambda c: y_b[:, c * 512: c * 512 + tl], 1e-5, x2t, tl)
        qt = []
        for chb in range(NCHB):
            q1 = k.emit("V", lambda e, c=chb, tl=tl: e.tensor_sub(
                tq1[:, 0:tl], x2_b[:, c * 512: c * 512 + tl], stm[:, 0:tl]),
                deps=[r5])
            q2 = k.emit("V", lambda e, tl=tl: e.tensor_mul(
                tq1[:, 0:tl], tq1[:, 0:tl], strr[:, 0:tl]), deps=[q1])
            q3 = k.emit("V", lambda e, tl=tl: e.tensor_scalar(
                tq1[:, 0:tl], tq1[:, 0:tl], invs[:, 0:1], None, Alu.mult),
                deps=[q2])
            q4 = k.emit("V", lambda e, tl=tl: e.tensor_scalar(
                tq1[:, 0:tl], tq1[:, 0:tl], -1000.0, 1000.0, Alu.max, Alu.min),
                deps=[q3])
            q5 = k.emit("V", lambda e, tl=tl: e.tensor_copy(
                tqi[:, 0:tl], tq1[:, 0:tl]), deps=[q4])
            q6 = k.emit("V", lambda e, tl=tl: e.tensor_copy(
                tq2[:, 0:tl], tqi[:, 0:tl]), deps=[q5])
            q7 = k.emit("V", lambda e, tl=tl: e.tensor_sub(
                tq3[:, 0:tl], tq1[:, 0:tl], tq2[:, 0:tl]), deps=[q6])
            q8 = k.emit("V", lambda e, tl=tl: e.tensor_scalar(
                tq4[:, 0:tl], tq3[:, 0:tl], 0.5, None, Alu.is_gt), deps=[q7])
            q9 = k.emit("V", lambda e, tl=tl: e.tensor_add(
                tq2[:, 0:tl], tq2[:, 0:tl], tq4[:, 0:tl]), deps=[q8])
            qa = k.emit("V", lambda e, tl=tl: e.tensor_scalar(
                tq4[:, 0:tl], tq3[:, 0:tl], -0.5, None, Alu.is_lt), deps=[q9])
            qb = k.emit("V", lambda e, tl=tl: e.tensor_sub(
                tq2[:, 0:tl], tq2[:, 0:tl], tq4[:, 0:tl]), deps=[qa])
            qt.append(pdump("pq", y_b, chb, t0, tl, k.emit(
                "V", lambda e, c=chb, tl=tl: e.tensor_scalar(
                    y_b[:, c * 512: c * 512 + tl], tq2[:, 0:tl], -127.0, 127.0,
                    Alu.max, Alu.min), deps=[qb])))
        ftok = []
        for cc in range(NCHB):
            bk = 4 + cc % 2
            mmt = None
            for kc in range(NCHB):
                mmt = k.emit("T", lambda e, kc=kc, cc=cc, tl=tl, bk=bk,
                             st=(kc == 0): e.matmul(
                    pb[bk][:, 0:tl],
                    fwb[:, kc * C + cc * 128: kc * C + cc * 128 + 128],
                    y_b[:, kc * 512: kc * 512 + tl],
                    start=st, stop=(kc == NCHB - 1)),
                    deps=[qt[kc], psfree.get(bk)])
            ev = k.emit("V", lambda e, cc=cc, tl=tl, bk=bk: e.tensor_scalar(
                ot_b[:, cc * 512: cc * 512 + tl], pb[bk][:, 0:tl],
                scol[:, 0:1], pcols["fbias"][:, cc: cc + 1], Alu.mult, Alu.add),
                deps=[mmt])
            psfree[bk] = ev
            ftok.append(pdump("pf", ot_b, cc, t0, tl, ev))
        r5 = ln_stats(lambda c: ot_b[:, c * 512: c * 512 + tl], 1e-5, ftok, tl)
        for chb in range(NCHB):
            s1 = k.emit("V", lambda e, c=chb, tl=tl: e.tensor_sub(
                tq1[:, 0:tl], ot_b[:, c * 512: c * 512 + tl], stm[:, 0:tl]),
                deps=[r5])
            s2 = k.emit("V", lambda e, tl=tl: e.tensor_mul(
                tq1[:, 0:tl], tq1[:, 0:tl], strr[:, 0:tl]), deps=[s1])
            s3 = k.emit("V", lambda e, c=chb, tl=tl: e.tensor_add(
                x2_b[:, c * 512: c * 512 + tl], x2_b[:, c * 512: c * 512 + tl],
                tq1[:, 0:tl]), deps=[s2])
            s3 = k.emit("V", lambda e, c=chb, tl=tl: e.tensor_copy(
                tq4[:, 0:tl], x2_b[:, c * 512: c * 512 + tl]), deps=[s3])
            prev_tile = k.emit("G", lambda e, c=chb, t0=t0, tl=tl: e.dma_start(
                bass.AP(out_t, c * 128 * TC + (t0 - W), [[TC, 128], [1, tl]]),
                x2_b[:, c * 512: c * 512 + tl]), deps=[s3], dma="SD")


_NC_CACHE = {}


def kernel(**inputs):
    p = "L0_attn_"
    x = np.asarray(inputs[p + "queries"], np.float32)[0]
    bf = ml_dtypes.bfloat16

    def padE(a, axis):
        pad = [(0, 0), (0, 0)]
        pad[axis] = (0, EP - E)
        return np.ascontiguousarray(np.pad(np.asarray(a, np.float32), pad))

    def padEv(v):
        return np.pad(np.asarray(v, np.float32), (0, EP - E))

    com = {
        "rW1": padE(inputs[p + "rW1"], 1), "rW2": padE(inputs[p + "rW2"], 0),
        "kW1": padE(inputs[p + "kW1"], 1), "kW2": padE(inputs[p + "kW2"], 0),
        "vW1": padE(inputs[p + "vW1"], 1), "vW2": padE(inputs[p + "vW2"], 0),
        "oW1": padE(inputs[p + "oW1"], 1), "oW2": padE(inputs[p + "oW2"], 0),
        "gA": np.asarray(inputs[p + "gA"], np.float32),
        "gB": np.asarray(inputs[p + "gB"], np.float32),
        "aA": np.asarray(inputs[p + "aA"], np.float32),
        "aB": np.asarray(inputs[p + "aB"], np.float32),
        "dA": np.asarray(inputs[p + "dA"], np.float32),
        "dB": np.asarray(inputs[p + "dB"], np.float32),
        "fWT": np.ascontiguousarray(
            np.asarray(inputs["L0_ffn_ternary_weight"]).astype(np.float32).T),
        "rb1": padEv(inputs[p + "rb1"]), "kb1": padEv(inputs[p + "kb1"]),
        "vb1": padEv(inputs[p + "vb1"]), "ob1": padEv(inputs[p + "ob1"]),
        "act_scale": np.asarray(inputs["L0_ffn_act_scale"],
                                np.float32).reshape(1, 1),
        "fbias": np.asarray(inputs["L0_ffn_bias"], np.float32),
    }
    for n in ["mu_r", "mu_k", "mu_v", "mu_g", "mu_a", "mu_d", "rb2", "kb2",
              "vb2", "ob2", "ab", "db", "rem_mult", "iclr_mix", "bonus_mult",
              "ln_g", "ln_b"]:
        com[n] = np.asarray(inputs[p + n], np.float32)
    ec = np.zeros((128, 4), np.float32)
    ec[:, 1] = 1e-5
    ec[:, 2] = 1e-6
    com["epscol"] = ec
    idn2 = np.concatenate([np.eye(64), np.eye(64)], 0)
    com["idn64"] = idn2.astype(bf)
    com["idn128"] = np.eye(128, dtype=bf)
    sl = np.tril(np.ones((128, 128), np.float32), -1)
    com["mSLn"] = -sl
    com["mSUn"] = np.ascontiguousarray(-sl.T)
    com["mSU"] = np.ascontiguousarray(sl.T)
    ui = np.triu(np.ones((128, 128), np.float32), 0)
    com["mUIn"] = -ui
    com["mUI"] = ui
    bones = np.zeros((128, 128), np.float32)
    bones[:64, :64] = 1.0
    bones[64:, 64:] = 1.0
    com["bones"] = bones.astype(bf)
    com["ones128"] = np.ones((128, 128), dtype=bf)

    if "nc" not in _NC_CACHE:
        _NC_CACHE["nc"] = build_nc()
    nc = _NC_CACHE["nc"]

    in_maps = []
    for i in range(NCORES):
        if i == 0:
            xs = np.concatenate([np.zeros((W, C), np.float32), x[0:TC]], 0)
            wm = np.zeros((128, W), np.float32)
        else:
            xs = x[i * TC - W: i * TC + TC]
            wm = np.ones((128, W), np.float32)
        m = dict(com)
        m["xT"] = np.ascontiguousarray(xs.T)
        m["wmask"] = wm
        in_maps.append(m)

    res = run_bass_kernel_spmd(nc, in_maps, core_ids=list(range(NCORES)))
    shards = [np.ascontiguousarray(res.results[i]["out"].T)
              for i in range(NCORES)]
    out = np.concatenate(shards, 0)[None]
    if DEBUG:
        kernel._dbg = res.results
    return out


# version 11
# speedup vs baseline: 1.1614x; 1.1614x over previous
import numpy as np
import ml_dtypes
import concourse.bass as bass
import concourse.mybir as mybir
from concourse.bass_utils import run_bass_kernel_spmd

F32 = mybir.dt.float32
BF16 = mybir.dt.bfloat16
I32 = mybir.dt.int32
Alu = mybir.AluOpType
Act = mybir.ActivationFunctionType

NCORES = 8
T = 16384
TC = 2048
W = 256
TT = TC + W
C = 1024
NCHB = 8
H = 16
N = 64
L = 128
NCH = TT // L
NWARM = W // L
E = 1228
EP = 1280
NEB = 10
DL = 128
DECAY_C = 0.7408182206817179

A_TILES = [(0, 512), (512, 512), (1024, 512), (1536, 512), (2048, 256)]
P_TILES = [(256, 512), (768, 512), (1280, 512), (1792, 512)]
R_GROUPS = [(2 * i, 2) for i in range(9)]

DEBUG = False
STOP_AFTER = "ALL"
kernel_SERIAL_A1 = False
R1_C = 2
R1_H = 0

_uniq = [0]


def uname(n):
    _uniq[0] += 1
    return f"{n}_{_uniq[0]}"


class K:
    ENG = ("T", "V", "A", "G", "S")
    DMAC = ("LD", "LD2", "SD")

    def __init__(self, nc):
        self.nc = nc
        self.instrs = {e: [] for e in self.ENG}
        self.cnt = {e: 0 for e in self.ENG}
        self.unit = {e: 1 for e in self.ENG}
        for c in self.DMAC:
            self.cnt[c] = 0
            self.unit[c] = 16
        self.sems = {}

    def make_sems(self, stack):
        for name in self.ENG + self.DMAC:
            self.sems[name] = stack.enter_context(self.nc.semaphore("sem" + name))

    def emit(self, e, fn, deps=(), inc=True, dma=None):
        deps = tuple((dk, self.cnt[dk]) if dk in self.DMAC else (dk, dv)
                     for (dk, dv) in (d for d in deps if d is not None))
        self.instrs[e].append((fn, deps, inc, dma))
        if dma is not None:
            self.cnt[dma] += 1
            return (dma, self.cnt[dma])
        if inc:
            self.cnt[e] += 1
            return (e, self.cnt[e])
        return None

    def engtok(self, *engs):
        return [(e, self.cnt[e]) for e in engs]

    def barrier(self):
        return [(e, self.cnt[e]) for e in ("T", "V", "A", "G")] + \
               [(c, self.cnt[c]) for c in self.DMAC]

    def realize(self):
        nc = self.nc
        sems = self.sems
        unit = self.unit
        self.trace = {e: [] for e in self.ENG}

        def run(engine, key):
            waited = {}
            for fn, deps, inc, dma in self.instrs[key]:
                for (dk, dv) in deps:
                    if dv <= 0 or dk == key:
                        continue
                    if waited.get(dk, 0) < dv:
                        engine.wait_ge(sems[dk], dv * unit[dk])
                        waited[dk] = dv
                        self.trace[key].append(("W", dk, dv))
                inst = fn(engine)
                self.trace[key].append(("I", dma or "", inc))
                if dma is not None:
                    inst.then_inc(sems[dma], 16)
                elif inc:
                    inst.then_inc(sems[key], 1)

        with nc.Block() as block:
            @block.tensor
            def _(t):
                run(t, "T")

            @block.vector
            def _(v):
                run(v, "V")

            @block.scalar
            def _(a):
                run(a, "A")

            @block.gpsimd
            def _(g):
                run(g, "G")
                g.wait_ge(sems["SD"], self.cnt["SD"] * 16)

            @block.sync
            def _(s):
                run(s, "S")
                s.wait_ge(sems["LD"], self.cnt["LD"] * 16)
                s.wait_ge(sems["LD2"], self.cnt["LD2"] * 16)


def build_nc():
    nc = bass.Bass("TRN2", target_bir_lowering=False)
    k = K(nc)
    build_nc.k = k

    xT = nc.dram_tensor("xT", [C, TT], F32, kind="ExternalInput")
    wmask = nc.dram_tensor("wmask", [128, W], F32, kind="ExternalInput")
    out_t = nc.dram_tensor("out", [C, TC], F32, kind="ExternalOutput")

    WNAMES = [
        ("rW1", [C, EP]), ("rW2", [EP, C]), ("kW1", [C, EP]), ("kW2", [EP, C]),
        ("vW1", [C, EP]), ("vW2", [EP, C]), ("oW1", [C, EP]), ("oW2", [EP, C]),
        ("gA", [C, DL]), ("gB", [DL, C]), ("aA", [C, DL]), ("aB", [DL, C]),
        ("dA", [C, DL]), ("dB", [DL, C]), ("fWT", [C, C]),
    ]
    wt = {n: nc.dram_tensor(n, s, F32, kind="ExternalInput") for n, s in WNAMES}
    PNAMES_C = ["mu_r", "mu_k", "mu_v", "mu_g", "mu_a", "mu_d", "rb2", "kb2",
                "vb2", "ob2", "ab", "db", "rem_mult", "iclr_mix", "bonus_mult",
                "ln_g", "ln_b", "fbias"]
    PNAMES_E = ["rb1", "kb1", "vb1", "ob1"]
    pt = {n: nc.dram_tensor(n, [C], F32, kind="ExternalInput") for n in PNAMES_C}
    pt |= {n: nc.dram_tensor(n, [EP], F32, kind="ExternalInput") for n in PNAMES_E}
    scale_t = nc.dram_tensor("act_scale", [1, 1], F32, kind="ExternalInput")
    epscol_t = nc.dram_tensor("epscol", [128, 4], F32, kind="ExternalInput")
    CONSTS = [("idn64", [128, 64], BF16), ("idn128", [128, 128], BF16),
              ("mSLn", [128, 128], F32), ("mSUn", [128, 128], F32),
              ("mSU", [128, 128], F32), ("mUIn", [128, 128], F32),
              ("mUI", [128, 128], F32),
              ("bones", [128, 128], BF16), ("ones128", [128, 128], BF16)]
    ct = {n: nc.dram_tensor(n, s, d, kind="ExternalInput") for n, s, d in CONSTS}

    def streams(name, width, dtype=BF16):
        return [nc.dram_tensor(f"{name}{i}", [128, width], dtype) for i in range(NCHB)]

    kst_h = streams("k_h", TT)
    vst_h = streams("v_h", TT)
    a_h = streams("a_h", TT)
    lcl_h = streams("lcl_h", TT, F32)
    ld_h = streams("ld_h", TT)
    r_h = streams("r_h", TC)
    gate_h = streams("gate_h", TC)
    Af_h = streams("Af_h", TT)
    Kc_h = streams("Kc_h", TT)
    Vc_h = streams("Vc_h", TT)
    Wv_h = streams("Wv_h", TT)
    Wkn_h = streams("Wkn_h", TT)
    RK_h = streams("RK_h", TT)
    cfm_h = streams("cfm_h", TC)
    bon_h = streams("bon_h", TC)
    mo_h = streams("mo_h", TC)

    dbg = {}
    if DEBUG:
        for nm, st in (("kk", kst_h), ("vv", vst_h), ("aa", a_h),
                       ("rr", r_h), ("Af", Af_h), ("Kc", Kc_h), ("RK", RK_h),
                       ("cf", cfm_h), ("bo", bon_h), ("mo", mo_h), ("ga", gate_h),
                       ("lc", lcl_h), ("Wc", Wv_h), ("Wk", Wkn_h),
                       ("Vc", Vc_h)):
            wdt = st[0].shape[1]
            dbg[nm] = (nc.dram_tensor("dbg_" + nm, [C, wdt], F32,
                                      kind="ExternalOutput"), st)

    from contextlib import ExitStack
    stack = ExitStack()
    k.make_sems(stack)

    def sb(name, shape, dtype):
        return stack.enter_context(nc.sbuf_tensor(name, shape, dtype))

    pb = [stack.enter_context(nc.psum_tensor(f"pb{i}", [128, 512], F32))
          for i in range(8)]

    cs = {}
    for n, s, d in CONSTS:
        cs[n] = sb("c_" + n, s, d)
        k.emit("S", lambda e, t=ct[n], b=cs[n]: e.dma_start(b[:, :], t[:, :]),
               dma="LD")
    wm_s = sb("c_wmask", [128, W], F32)
    k.emit("S", lambda e: e.dma_start(wm_s[:, :], wmask[:, :]), dma="LD")

    pcols = {}
    for n in PNAMES_C:
        pcols[n] = sb("p_" + n, [128, NCHB], F32)
        k.emit("S", lambda e, t=pt[n], b=pcols[n]: e.dma_start(
            b[:, :], bass.AP(t, 0, [[1, 128], [128, NCHB]]),
            allow_slow_non_contiguous=True), dma="LD")
    for n in PNAMES_E:
        pcols[n] = sb("p_" + n, [128, NEB], F32)
        k.emit("S", lambda e, t=pt[n], b=pcols[n]: e.dma_start(
            b[:, :], bass.AP(t, 0, [[1, 128], [128, NEB]]),
            allow_slow_non_contiguous=True), dma="LD")
    sc_s = sb("p_scale", [1, 1], F32)
    k.emit("S", lambda e: e.dma_start(sc_s[:, :], scale_t[:, :]), dma="LD")
    ec_s = sb("p_epscol", [128, 4], F32)
    k.emit("S", lambda e: e.dma_start(ec_s[:, :], epscol_t[:, :]), dma="LD")
    consts_ready = k.barrier()
    pcols["one_m_mix"] = sb("p_omm", [128, NCHB], F32)
    t_omm = k.emit("V", lambda e: e.tensor_scalar(
        pcols["one_m_mix"][:, :], pcols["iclr_mix"][:, :], -1.0, 1.0,
        Alu.mult, Alu.add), deps=consts_ready)

    bf_mu = sb("bf_mu", [128, NCHB], BF16)
    cvt = sb("cvt", [128, 512], BF16)
    zz_s = sb("zz_s", [128, 128], F32)
    cL_s = sb("cL_s", [128, NCHB * NCH], F32)

    from contextlib import ExitStack as _ES
    stackQ = _ES()
    q_s = stackQ.enter_context(nc.sbuf_tensor(uname("q_s"), [128, NCHB * TT], BF16))

    def q_sl(chb, t0, tl):
        return q_s[:, chb * TT + t0: chb * TT + t0 + tl]

    with nc.sbuf_tensor(uname("xa"), [128, NCHB * 512], F32) as xa, \
         nc.sbuf_tensor(uname("xb"), [128, NCHB * 512], BF16) as xb, \
         nc.sbuf_tensor(uname("x2b"), [128, NCHB * 512], BF16) as x2b, \
         nc.sbuf_tensor(uname("stm"), [128, 512], F32) as stm, \
         nc.sbuf_tensor(uname("str1"), [128, 512], F32) as strr:
        prev_ev = None
        for (t0, tl) in A_TILES:
            lds = []
            _bar = k.barrier() if kernel_SERIAL_A1 else None
            for chb in range(NCHB):
                lds.append(k.emit("S", lambda e, c=chb, t0=t0, tl=tl: e.dma_start(
                    xa[:, c * 512: c * 512 + tl],
                    bass.AP(xT, c * 128 * TT + t0, [[TT, 128], [1, tl]])),
                    deps=consts_ready + [prev_ev] + (_bar or []), dma="LD"))
            casts, sqs = [], []
            for chb in range(NCHB):
                casts.append(k.emit("V", lambda e, c=chb, tl=tl: e.tensor_copy(
                    xb[:, c * 512: c * 512 + tl], xa[:, c * 512: c * 512 + tl]),
                    deps=[lds[chb], prev_ev]))
                sqs.append(k.emit("V", lambda e, c=chb, tl=tl: e.tensor_mul(
                    x2b[:, c * 512: c * 512 + tl], xb[:, c * 512: c * 512 + tl],
                    xb[:, c * 512: c * 512 + tl]), deps=[casts[chb]]))
            mm1 = mm2 = None
            for chb in range(NCHB):
                mm1 = k.emit("T", lambda e, c=chb, tl=tl: e.matmul(
                    pb[6][:, 0:tl], cs["ones128"][:, :],
                    xb[:, c * 512: c * 512 + tl],
                    start=(c == 0), stop=(c == NCHB - 1)),
                    deps=[casts[chb], prev_ev])
            for chb in range(NCHB):
                mm2 = k.emit("T", lambda e, c=chb, tl=tl: e.matmul(
                    pb[7][:, 0:tl], cs["ones128"][:, :],
                    x2b[:, c * 512: c * 512 + tl],
                    start=(c == 0), stop=(c == NCHB - 1)), deps=[sqs[chb]])
            e1 = k.emit("V", lambda e, tl=tl: e.tensor_scalar(
                stm[:, 0:tl], pb[6][:, 0:tl], 1.0 / C, None, Alu.mult), deps=[mm1])
            e2 = k.emit("V", lambda e, tl=tl: e.tensor_mul(
                strr[:, 0:tl], stm[:, 0:tl], stm[:, 0:tl]), deps=[e1])
            e3 = k.emit("V", lambda e, tl=tl: e.scalar_tensor_tensor(
                strr[:, 0:tl], pb[7][:, 0:tl], 1.0 / C, strr[:, 0:tl],
                Alu.mult, Alu.subtract), deps=[mm2, e2])
            e4 = k.emit("A", lambda e, tl=tl: e.activation(
                strr[:, 0:tl], strr[:, 0:tl], Act.Ln, bias=ec_s[:, 1:2], scale=1.0),
                deps=[e3])
            e5 = k.emit("A", lambda e, tl=tl: e.activation(
                strr[:, 0:tl], strr[:, 0:tl], Act.Exp, bias=ec_s[:, 0:1], scale=-0.5),
                deps=[e4])
            for chb in range(NCHB):
                s1 = k.emit("V", lambda e, c=chb, tl=tl: e.tensor_sub(
                    xb[:, c * 512: c * 512 + tl], xb[:, c * 512: c * 512 + tl],
                    stm[:, 0:tl]), deps=[e5])
                prev_ev = k.emit("V", lambda e, c=chb, t0=t0, tl=tl: e.tensor_mul(
                    q_sl(c, t0, tl), xb[:, c * 512: c * 512 + tl], strr[:, 0:tl]),
                    deps=[s1])
        a1_done = k.barrier()

    if DEBUG:
        q_d = nc.dram_tensor("dbg_q", [C, TT], F32, kind="ExternalOutput")
        with nc.sbuf_tensor(uname("qdf"), [128, TT], F32) as qdf:
            pq = None
            for chb in range(NCHB):
                cv = k.emit("V", lambda e, c=chb: e.tensor_copy(
                    qdf[:, :], q_s[:, c * TT:(c + 1) * TT]), deps=a1_done + [pq])
                pq = k.emit("G", lambda e, c=chb: e.dma_start(
                    bass.AP(q_d, c * 128 * TT, [[TT, 128], [1, TT]]), qdf[:, :]),
                    deps=[cv], dma="SD")
            qdump_done = k.barrier()
        if STOP_AFTER == "A1":
            k.realize()
            stackQ.close()
            stack.close()
            return nc

    mu_state = {"tok": None}

    def set_mu(name, deps):
        mu_state["tok"] = k.emit("V", lambda e, nm=name: e.tensor_copy(
            bf_mu[:, :], pcols[nm][:, :]), deps=deps)

    def mlp(dst_h, W1d, b1col, W2d, b2col, tok_range, deps0):
        dst_off = TT - dst_h[0].shape[1]
        with nc.sbuf_tensor(uname("w1b"), [128, NCHB * EP], BF16) as w1b, \
             nc.sbuf_tensor(uname("w2b"), [128, NEB * C], BF16) as w2b, \
             nc.sbuf_tensor(uname("wld"), [128, EP], F32) as wld, \
             nc.sbuf_tensor(uname("b1c"), [128, NEB], F32) as b1c, \
             nc.sbuf_tensor(uname("hb"), [128, NEB * 512], BF16) as hb, \
             nc.sbuf_tensor(uname("ob"), [128, NCHB * 512], BF16) as ob:
            w1tok = []
            for kc in range(NCHB):
                ld = k.emit("S", lambda e, kc=kc, W1d=W1d: e.dma_start(
                    wld[:, 0:EP], bass.AP(W1d, kc * 128 * EP, [[EP, 128], [1, EP]])),
                    deps=list(deps0) + ([w1tok[-1]] if w1tok else []), dma="LD")
                w1tok.append(k.emit("V", lambda e, kc=kc: e.tensor_copy(
                    w1b[:, kc * EP: kc * EP + EP], wld[:, 0:EP]), deps=[ld]))
            mmt = None
            for ec in range(NEB):
                for kc in range(NCHB):
                    mmt = k.emit("T", lambda e, kc=kc, ec=ec,
                                 st=(ec == 0 and kc == 0),
                                 sp=(ec == NEB - 1 and kc == NCHB - 1): e.matmul(
                        pb[5][:, ec: ec + 1],
                        w1b[:, kc * EP + ec * 128: kc * EP + ec * 128 + 128],
                        bf_mu[:, kc: kc + 1], start=st, stop=sp),
                        deps=[w1tok[kc], mu_state["tok"]])
            badd = k.emit("V", lambda e, b1col=b1col: e.tensor_tensor(
                b1c[:, :], pb[5][:, 0:NEB], b1col[:, :], Alu.add), deps=[mmt])
            w2tok = []
            for ec in range(NEB):
                ld = k.emit("S", lambda e, ec=ec, W2d=W2d: e.dma_start(
                    wld[:, 0:C], bass.AP(W2d, ec * 128 * C, [[C, 128], [1, C]])),
                    deps=[w1tok[-1]] + ([w2tok[-1]] if w2tok else []), dma="LD")
                w2tok.append(k.emit("V", lambda e, ec=ec: e.tensor_copy(
                    w2b[:, ec * C: ec * C + C], wld[:, 0:C]), deps=[ld]))
            tstart = tok_range[0]
            tiles = [(max(t0, tstart), t0 + tl - max(t0, tstart))
                     for (t0, tl) in A_TILES if t0 + tl > tstart]
            bank_ev = {0: None, 1: None, 2: None, 3: None}
            prev_store = None
            tile_store = None
            for (t0, tl) in tiles:
                htok = [None] * NEB
                for ec in range(NEB):
                    bk = ec % 2
                    mmt = None
                    for kc in range(NCHB):
                        mmt = k.emit("T", lambda e, kc=kc, ec=ec, t0=t0, tl=tl,
                                     bk=bk, st=(kc == 0): e.matmul(
                            pb[bk][:, 0:tl],
                            w1b[:, kc * EP + ec * 128: kc * EP + ec * 128 + 128],
                            q_sl(kc, t0, tl), start=st, stop=(kc == NCHB - 1)),
                            deps=[w1tok[kc], badd, bank_ev[bk]])
                    htok[ec] = k.emit("A", lambda e, ec=ec, tl=tl, bk=bk:
                                      e.activation(
                        hb[:, ec * 512: ec * 512 + tl], pb[bk][:, 0:tl], Act.Gelu,
                        bias=b1c[:, ec: ec + 1], scale=1.0), deps=[mmt])
                    bank_ev[bk] = htok[ec]
                for cc in range(NCHB):
                    bk = 2 + cc % 2
                    mmt = None
                    for ec in range(NEB):
                        mmt = k.emit("T", lambda e, ec=ec, cc=cc, tl=tl, bk=bk,
                                     st=(ec == 0): e.matmul(
                            pb[bk][:, 0:tl],
                            w2b[:, ec * C + cc * 128: ec * C + cc * 128 + 128],
                            hb[:, ec * 512: ec * 512 + tl],
                            start=st, stop=(ec == NEB - 1)),
                            deps=[htok[ec], w2tok[ec], bank_ev[bk]])
                    ev = k.emit("V", lambda e, cc=cc, tl=tl, bk=bk, b2col=b2col:
                                e.tensor_scalar(
                        ob[:, cc * 512: cc * 512 + tl], pb[bk][:, 0:tl],
                        b2col[:, cc: cc + 1],
                        None, Alu.add),
                        deps=[mmt, tile_store if cc == 0 else None])
                    bank_ev[bk] = ev
                    prev_store = k.emit("G", lambda e, cc=cc, t0=t0, tl=tl,
                                        dst=dst_h, off=dst_off: e.dma_start(
                        dst[cc][:, t0 - off: t0 - off + tl],
                        ob[:, cc * 512: cc * 512 + tl]),
                        deps=[ev], dma="SD")
                tile_store = prev_store
        return k.barrier()

    set_mu("mu_k", a1_done + [t_omm])
    d_k = mlp(kst_h, wt["kW1"], pcols["kb1"], wt["kW2"], pcols["kb2"], (0,), a1_done)
    set_mu("mu_v", d_k)
    d_v = mlp(vst_h, wt["vW1"], pcols["vb1"], wt["vW2"], pcols["vb2"], (0,), d_k)
    set_mu("mu_r", d_v)
    d_r = mlp(r_h, wt["rW1"], pcols["rb1"], wt["rW2"], pcols["rb2"], (W,), d_v)
    a2_done = d_r

    cvt_state = {"tok": None}

    def lowrank(A_t, B_t, bvec, tok_range, deps0, post):
        with nc.sbuf_tensor(uname("lAb"), [128, C], BF16) as lAb, \
             nc.sbuf_tensor(uname("lBb"), [128, C], BF16) as lBb, \
             nc.sbuf_tensor(uname("lld"), [128, C], F32) as lld, \
             nc.sbuf_tensor(uname("lbc"), [128, NCHB], F32) as lbc, \
             nc.sbuf_tensor(uname("lmid"), [128, 512], BF16) as lmid, \
             nc.sbuf_tensor(uname("lout"), [128, 512], F32) as lout, \
             nc.sbuf_tensor(uname("lo2"), [128, 512], F32) as lo2:
            atok = []
            for kc in range(NCHB):
                ld = k.emit("S", lambda e, kc=kc, A_t=A_t: e.dma_start(
                    lld[:, 0:DL], bass.AP(A_t, kc * 128 * DL, [[DL, 128], [1, DL]])),
                    deps=list(deps0) + ([atok[-1]] if atok else []), dma="LD")
                atok.append(k.emit("V", lambda e, kc=kc: e.tensor_copy(
                    lAb[:, kc * DL: kc * DL + DL], lld[:, 0:DL]), deps=[ld]))
            bld = k.emit("S", lambda e, B_t=B_t: e.dma_start(
                lld[:, 0:C], bass.AP(B_t, 0, [[C, 128], [1, C]])),
                deps=[atok[-1]], dma="LD")
            btok = k.emit("V", lambda e: e.tensor_copy(lBb[:, :], lld[:, 0:C]),
                          deps=[bld])
            mmt = None
            for kc in range(NCHB):
                mmt = k.emit("T", lambda e, kc=kc, st=(kc == 0): e.matmul(
                    pb[5][0:DL, 0:1], lAb[:, kc * DL: kc * DL + DL],
                    bf_mu[:, kc: kc + 1], start=st, stop=(kc == NCHB - 1)),
                    deps=[atok[kc], mu_state["tok"]])
            muAev = k.emit("V", lambda e: e.tensor_copy(
                lmid[0:DL, 0:1], pb[5][0:DL, 0:1]), deps=[mmt])
            mmt = None
            for cc in range(NCHB):
                mmt = k.emit("T", lambda e, cc=cc, st=(cc == 0): e.matmul(
                    pb[5][:, 4 + cc: 5 + cc], lBb[:, cc * 128: cc * 128 + 128],
                    lmid[0:DL, 0:1], start=st, stop=(cc == NCHB - 1)),
                    deps=[muAev, btok])
            if bvec is not None:
                bc = k.emit("V", lambda e, bvec=bvec: e.tensor_tensor(
                    lbc[:, :], pb[5][:, 4:4 + NCHB], bvec[:, :], Alu.add), deps=[mmt])
            else:
                bc = k.emit("V", lambda e: e.tensor_copy(
                    lbc[:, :], pb[5][:, 4:4 + NCHB]), deps=[mmt])
            tstart = tok_range[0]
            tiles = [(max(t0, tstart), t0 + tl - max(t0, tstart))
                     for (t0, tl) in A_TILES if t0 + tl > tstart]
            prev_mid = None
            prev_out = None
            for (t0, tl) in tiles:
                mmt = None
                for kc in range(NCHB):
                    mmt = k.emit("T", lambda e, kc=kc, t0=t0, tl=tl, st=(kc == 0):
                                 e.matmul(pb[0][0:DL, 0:tl],
                                          lAb[:, kc * DL: kc * DL + DL],
                                          q_sl(kc, t0, tl),
                                          start=st, stop=(kc == NCHB - 1)),
                                 deps=[atok[kc], prev_mid, a1_done[0]])
                mev = k.emit("V", lambda e, tl=tl: e.tensor_copy(
                    lmid[0:DL, 0:tl], pb[0][0:DL, 0:tl]), deps=[mmt])
                prev_mid = mev
                for cc in range(NCHB):
                    m2 = k.emit("T", lambda e, cc=cc, tl=tl: e.matmul(
                        pb[1][:, 0:tl], lBb[:, cc * 128: cc * 128 + 128],
                        lmid[0:DL, 0:tl], start=True, stop=True),
                        deps=[mev, btok, prev_out])
                    prev_out = post(cc, t0, tl, m2, lbc, lout, lo2)
            res = k.barrier()
        return res

    def post_sig(dst, off):
        def post(cc, t0, tl, m2, lbc, lout, lo2):
            ev = k.emit("A", lambda e, cc=cc, tl=tl, lbc=lbc, lout=lout:
                        e.activation(lout[:, 0:tl], pb[1][:, 0:tl], Act.Sigmoid,
                                     bias=lbc[:, cc: cc + 1], scale=1.0), deps=[m2])
            cv = k.emit("V", lambda e, tl=tl, lout=lout: e.tensor_copy(
                cvt[:, 0:tl], lout[:, 0:tl]), deps=[ev, cvt_state["tok"]])
            st = k.emit("G", lambda e, cc=cc, t0=t0, tl=tl, dst=dst, off=off:
                        e.dma_start(dst[cc][:, t0 - off: t0 - off + tl],
                                    cvt[:, 0:tl]), deps=[cv], dma="SD")
            cvt_state["tok"] = st
            return st
        return post

    set_mu("mu_a", a2_done)
    d_a = lowrank(wt["aA"], wt["aB"], pcols["ab"], (0,), a2_done,
                  post_sig(a_h, 0))
    set_mu("mu_g", d_a)
    d_g = lowrank(wt["gA"], wt["gB"], None, (W,), d_a, post_sig(gate_h, W))

    z_ini = k.emit("V", lambda e: e.memset(zz_s[:, :], 0.0), deps=d_g)
    with nc.sbuf_tensor(uname("ldf"), [128, 512], F32) as ldf, \
         nc.sbuf_tensor(uname("lcf"), [128, 512], F32) as lcf:
        def post_d(cc, t0, tl, m2, lbc, lout, lo2):
            ev0 = k.emit("A", lambda e, cc=cc, tl=tl, lbc=lbc, lo2=lo2:
                         e.activation(lo2[:, 0:tl], pb[1][:, 0:tl], Act.Tanh,
                                      bias=lbc[:, cc: cc + 1], scale=1.0), deps=[m2])
            ev = k.emit("A", lambda e, tl=tl, lo2=lo2, lout=lout: e.activation(
                lout[:, 0:tl], lo2[:, 0:tl], Act.Sigmoid, bias=ec_s[:, 0:1], scale=1.0),
                deps=[ev0])
            sld = k.emit("V", lambda e, tl=tl, lout=lout: e.tensor_scalar(
                ldf[:, 0:tl], lout[:, 0:tl], -DECAY_C, None, Alu.mult),
                deps=[ev, cvt_state["tok"]])
            cb = k.emit("V", lambda e, tl=tl: e.tensor_copy(
                cvt[:, 0:tl], ldf[:, 0:tl]), deps=[sld])
            st1 = k.emit("G", lambda e, cc=cc, t0=t0, tl=tl: e.dma_start(
                ld_h[cc][:, t0: t0 + tl], cvt[:, 0:tl]), deps=[cb], dma="SD")
            sc = None
            for s0 in range(0, tl, L):
                sc = k.emit("V", lambda e, s0=s0: e.tensor_tensor_scan(
                    lcf[:, s0: s0 + L], ldf[:, s0: s0 + L], zz_s[:, :], 0.0,
                    Alu.add, Alu.add), deps=[sld, z_ini])
            st2 = k.emit("G", lambda e, cc=cc, t0=t0, tl=tl: e.dma_start(
                lcl_h[cc][:, t0: t0 + tl], lcf[:, 0:tl]), deps=[sc], dma="SD")
            cvt_state["tok"] = st2
            return st2
        set_mu("mu_d", d_g)
        d_d = lowrank(wt["dA"], wt["dB"], pcols["db"], (0,), d_g, post_d)
    a3_done = d_d
    stackQ.close()

    with nc.sbuf_tensor(uname("t_k"), [128, 512], BF16) as t_k, \
         nc.sbuf_tensor(uname("t_v"), [128, 512], BF16) as t_v, \
         nc.sbuf_tensor(uname("t_a"), [128, 512], BF16) as t_a, \
         nc.sbuf_tensor(uname("t_lc"), [128, 512], F32) as t_lc, \
         nc.sbuf_tensor(uname("t_ld"), [128, 512], BF16) as t_ld, \
         nc.sbuf_tensor(uname("t_e1"), [128, 512], F32) as t_e1, \
         nc.sbuf_tensor(uname("t_e2"), [128, 512], F32) as t_e2, \
         nc.sbuf_tensor(uname("t_e3"), [128, 512], F32) as t_e3, \
         nc.sbuf_tensor(uname("t_cf"), [128, 512], F32) as t_cf, \
         nc.sbuf_tensor(uname("t_rm"), [128, 512], BF16) as t_rm, \
         nc.sbuf_tensor(uname("t_s1"), [128, 512], F32) as t_s1, \
         nc.sbuf_tensor(uname("t_s2"), [128, 512], BF16) as t_s2, \
         nc.sbuf_tensor(uname("t_kn"), [128, 512], BF16) as t_kn, \
         nc.sbuf_tensor(uname("t_o"), [128, 7 * 512], BF16) as t_o:
        prev_st = None
        for (t0, tl) in A_TILES:
            for chb in range(NCHB):
                lk = k.emit("S", lambda e, c=chb, t0=t0, tl=tl: e.dma_start(
                    t_k[:, 0:tl], kst_h[c][:, t0: t0 + tl]),
                    deps=a3_done + [prev_st], dma="LD")
                lv = k.emit("S", lambda e, c=chb, t0=t0, tl=tl: e.dma_start(
                    t_v[:, 0:tl], vst_h[c][:, t0: t0 + tl]), dma="LD")
                la = k.emit("S", lambda e, c=chb, t0=t0, tl=tl: e.dma_start(
                    t_a[:, 0:tl], a_h[c][:, t0: t0 + tl]), dma="LD")
                llc = k.emit("S", lambda e, c=chb, t0=t0, tl=tl: e.dma_start(
                    t_lc[:, 0:tl], lcl_h[c][:, t0: t0 + tl]), dma="LD")
                lld_ = k.emit("S", lambda e, c=chb, t0=t0, tl=tl: e.dma_start(
                    t_ld[:, 0:tl], ld_h[c][:, t0: t0 + tl]), dma="LD")
                rm = k.emit("V", lambda e, c=chb, tl=tl: e.tensor_scalar(
                    t_rm[:, 0:tl], t_k[:, 0:tl], pcols["rem_mult"][:, c: c + 1],
                    None, Alu.mult), deps=[lk])
                rm2 = k.emit("V", lambda e, tl=tl: e.tensor_mul(
                    t_s2[:, 0:tl], t_rm[:, 0:tl], t_rm[:, 0:tl]), deps=[rm])
                ssm = k.emit("T", lambda e, tl=tl: e.matmul(
                    pb[4][:, 0:tl], cs["bones"][:, :], t_s2[:, 0:tl],
                    start=True, stop=True), deps=[rm2, prev_st])
                sq = k.emit("A", lambda e, tl=tl: e.activation(
                    t_s1[:, 0:tl], pb[4][:, 0:tl], Act.Sqrt, bias=ec_s[:, 0:1], scale=1.0),
                    deps=[ssm])
                nm = k.emit("V", lambda e, tl=tl: e.tensor_scalar(
                    t_s1[:, 0:tl], t_s1[:, 0:tl], 1e-12, None, Alu.max), deps=[sq])
                il = k.emit("A", lambda e, tl=tl: e.activation(
                    t_s1[:, 0:tl], t_s1[:, 0:tl], Act.Ln, bias=ec_s[:, 0:1], scale=1.0),
                    deps=[nm])
                iv = k.emit("A", lambda e, tl=tl: e.activation(
                    t_s1[:, 0:tl], t_s1[:, 0:tl], Act.Exp, bias=ec_s[:, 0:1], scale=-1.0),
                    deps=[il])
                kn = k.emit("V", lambda e, tl=tl: e.tensor_mul(
                    t_kn[:, 0:tl], t_rm[:, 0:tl], t_s1[:, 0:tl]), deps=[iv])
                lvm = lv
                if t0 == 0:
                    kn = k.emit("V", lambda e: e.tensor_mul(
                        t_kn[:, 0:W], t_kn[:, 0:W], wm_s[:, :]), deps=[kn])
                    lvm = k.emit("V", lambda e: e.tensor_mul(
                        t_v[:, 0:W], t_v[:, 0:W], wm_s[:, :]), deps=[lv])
                s1 = k.emit("V", lambda e, tl=tl: e.tensor_sub(
                    t_s1[:, 0:tl], t_lc[:, 0:tl], t_ld[:, 0:tl]),
                    deps=[llc, lld_, iv])
                e1t = k.emit("A", lambda e, tl=tl: e.activation(
                    t_e1[:, 0:tl], t_s1[:, 0:tl], Act.Exp, bias=ec_s[:, 0:1], scale=1.0),
                    deps=[s1])
                e2t = k.emit("A", lambda e, tl=tl: e.activation(
                    t_e2[:, 0:tl], t_lc[:, 0:tl], Act.Exp, bias=ec_s[:, 0:1], scale=-1.0),
                    deps=[llc])
                cft = k.emit("A", lambda e, tl=tl: e.activation(
                    t_cf[:, 0:tl], t_lc[:, 0:tl], Act.Exp, bias=ec_s[:, 0:1], scale=1.0),
                    deps=[llc])
                ccps = []
                for s0 in range(0, tl, L):
                    ci = (t0 + s0) // L
                    ccps.append(k.emit("A", lambda e, s0=s0, ci=ci, c=chb:
                                       e.activation(
                        cL_s[:, c * NCH + ci: c * NCH + ci + 1],
                        t_cf[:, s0 + L - 1: s0 + L], Act.Copy,
                        bias=0.0, scale=1.0), deps=[cft]))
                e3t = None
                for si_, s0 in enumerate(range(0, tl, L)):
                    ci = (t0 + s0) // L
                    e3t = k.emit("V", lambda e, s0=s0, ci=ci, c=chb: e.tensor_scalar(
                        t_e3[:, s0: s0 + L], t_e2[:, s0: s0 + L],
                        cL_s[:, c * NCH + ci: c * NCH + ci + 1], None, Alu.mult),
                        deps=[e2t, ccps[si_]])

                def osl(i, tl=512):
                    return t_o[:, i * 512: i * 512 + tl]

                def store(dst, tok, i, c=chb, t0=t0, tl=tl, off=0):
                    return k.emit("G", lambda e, dst=dst, c=c, t0=t0, tl=tl,
                                  off=off, i=i:
                                  e.dma_start(dst[c][:, t0 - off: t0 - off + tl],
                                              osl(i, tl)), deps=[tok], dma="SD")
                x1 = k.emit("V", lambda e, tl=tl: e.tensor_mul(
                    t_s2[:, 0:tl], t_a[:, 0:tl], t_kn[:, 0:tl]), deps=[la, kn])
                x2 = k.emit("V", lambda e, tl=tl: e.tensor_mul(
                    osl(0, tl), t_s2[:, 0:tl], t_e1[:, 0:tl]), deps=[x1, e1t])
                store(Af_h, x2, 0)
                x3 = k.emit("V", lambda e, tl=tl: e.tensor_mul(
                    osl(1, tl), t_kn[:, 0:tl], t_e2[:, 0:tl]),
                    deps=[kn, e2t])
                store(Kc_h, x3, 1)
                x4 = k.emit("V", lambda e, tl=tl: e.tensor_mul(
                    osl(2, tl), t_v[:, 0:tl], t_e2[:, 0:tl]),
                    deps=[lvm, e2t])
                store(Vc_h, x4, 2)
                x5 = k.emit("V", lambda e, tl=tl: e.tensor_mul(
                    osl(3, tl), t_v[:, 0:tl], t_e3[:, 0:tl]),
                    deps=[lvm, e3t])
                store(Wv_h, x5, 3)
                x6 = k.emit("V", lambda e, tl=tl: e.scalar_tensor_tensor(
                    osl(4, tl), t_kn[:, 0:tl], -1.0, t_e3[:, 0:tl],
                    Alu.mult, Alu.mult), deps=[kn, e3t])
                store(Wkn_h, x6, 4)
                x7 = k.emit("V", lambda e, c=chb, tl=tl: e.tensor_scalar(
                    t_s2[:, 0:tl], t_a[:, 0:tl], pcols["iclr_mix"][:, c: c + 1],
                    pcols["one_m_mix"][:, c: c + 1], Alu.mult, Alu.add),
                    deps=[la])
                x8 = k.emit("V", lambda e, tl=tl: e.tensor_mul(
                    osl(5, tl), t_k[:, 0:tl], t_s2[:, 0:tl]), deps=[x7, lk])
                prev_st = store(RK_h, x8, 5)
                if t0 >= W:
                    x9 = k.emit("V", lambda e, tl=tl: e.tensor_copy(
                        osl(6, tl), t_cf[:, 0:tl]), deps=[cft])
                    prev_st = store(cfm_h, x9, 6, off=W)
                elif t0 == 0 and tl > W:
                    x9 = k.emit("V", lambda e, tl=tl: e.tensor_copy(
                        osl(6, tl - W), t_cf[:, W:tl]), deps=[cft])
                    prev_st = k.emit("G", lambda e, c=chb, tl=tl: e.dma_start(
                        cfm_h[c][:, 0:tl - W], osl(6, tl - W)),
                        deps=[x9], dma="SD")
    a4_done = k.barrier()

    with nc.sbuf_tensor(uname("b_r"), [128, 512], BF16) as b_r, \
         nc.sbuf_tensor(uname("b_rk"), [128, 512], BF16) as b_rk, \
         nc.sbuf_tensor(uname("b_v"), [128, 512], BF16) as b_v, \
         nc.sbuf_tensor(uname("b_t"), [128, 512], BF16) as b_t, \
         nc.sbuf_tensor(uname("b_o"), [128, 512], BF16) as b_o:
        prev = None
        for (t0, tl) in P_TILES:
            for chb in range(NCHB):
                lr = k.emit("S", lambda e, c=chb, t0=t0, tl=tl: e.dma_start(
                    b_r[:, 0:tl], r_h[c][:, t0 - W: t0 - W + tl]),
                    deps=a4_done + [prev], dma="LD")
                lrk = k.emit("S", lambda e, c=chb, t0=t0, tl=tl: e.dma_start(
                    b_rk[:, 0:tl], RK_h[c][:, t0: t0 + tl]), dma="LD")
                lv = k.emit("S", lambda e, c=chb, t0=t0, tl=tl: e.dma_start(
                    b_v[:, 0:tl], vst_h[c][:, t0: t0 + tl]), dma="LD")
                m1 = k.emit("V", lambda e, tl=tl: e.tensor_mul(
                    b_t[:, 0:tl], b_r[:, 0:tl], b_rk[:, 0:tl]),
                    deps=[lr, lrk, prev])
                mm = k.emit("T", lambda e, tl=tl: e.matmul(
                    pb[4][:, 0:tl], cs["bones"][:, :], b_t[:, 0:tl],
                    start=True, stop=True), deps=[m1, prev])
                s1 = k.emit("V", lambda e, c=chb, tl=tl: e.tensor_scalar(
                    b_o[:, 0:tl], pb[4][:, 0:tl], 0.2,
                    pcols["bonus_mult"][:, c: c + 1], Alu.mult, Alu.mult),
                    deps=[mm, lv])
                s2 = k.emit("V", lambda e, tl=tl: e.tensor_mul(
                    b_o[:, 0:tl], b_o[:, 0:tl], b_v[:, 0:tl]), deps=[s1])
                prev = k.emit("G", lambda e, c=chb, t0=t0, tl=tl: e.dma_start(
                    bon_h[c][:, t0 - W: t0 - W + tl], b_o[:, 0:tl]),
                    deps=[s2], dma="SD")
    a5_done = k.barrier()

    if STOP_AFTER == "A":
        emit_debug(nc, k, dbg)
        k.realize()
        stack.close()
        return nc

    STRS = [Af_h, Kc_h, Vc_h, Wv_h, Wkn_h, RK_h, r_h]
    NSTR = 7
    GW = 256
    stackR = _ES()

    def sbr(name, shape, dtype):
        return stackR.enter_context(nc.sbuf_tensor(uname(name), shape, dtype))
    rs = sbr("rs", [128, 2 * NSTR * NCHB * GW], BF16)
    z0 = sbr("z0", [64, 2 * 3 * H * GW], BF16)
    S0_st = sbr("S0_st", [64, H * 64], BF16)
    cL0 = sbr("cL0", [64, NCHB * NCH], F32)
    m_X = sbr("m_X", [128, 4 * 6 * 128], BF16)
    m_Y = sbr("m_Y", [128, 4 * 7 * 128], BF16)
    m_MvT = sbr("m_MvT", [128, 4 * 128], BF16)
    m_Prk = sbr("m_Prk", [128, 4 * 128], BF16)
    m_tm = sbr("m_tm", [128, 4 * 320], BF16)
    m_V = sbr("m_V", [128, 128], BF16)
    m_U = sbr("m_U", [128, 128], BF16)
    m_Ufm = sbr("m_Ufm", [64, 128], BF16)
    m_Put = sbr("m_Put", [128, 128], BF16)
    m_Z = sbr("m_Z", [64, 64], BF16)
    m_cd = sbr("m_cd", [64, 64], F32)
    m_mo = sbr("m_mo", [64, H * GW], BF16)
    m_scr = sbr("m_scr", [128, 128], F32)

    R_SETTLE = False

    def dup(eng, fn, deps):
        return k.emit(eng, fn, deps=deps)

    def settle(eng, pb_sl_fn, P, Wd, deps):
        if not R_SETTLE:
            return list(deps)
        if eng == "A":
            t = k.emit("A", lambda e, f=pb_sl_fn, P=P, Wd=Wd: e.activation(
                m_scr[0:P, 0:Wd], f(), Act.Copy, bias=0.0, scale=1.0),
                deps=deps)
        else:
            t = k.emit("V", lambda e, f=pb_sl_fn, P=P, Wd=Wd: e.tensor_copy(
                m_scr[0:P, 0:Wd], f()), deps=deps)
        return [t]

    if DEBUG:
        import sys as _sys
        for _nm, _t in (("rs", rs), ("z0", z0), ("S0_st", S0_st),
                        ("cL0", cL0), ("m_X", m_X), ("m_Y", m_Y),
                        ("m_mo", m_mo), ("m_scr", m_scr)):
            _m = nc.lookup_mloc(_t)
            print(f"RALLOC {_nm}: addr={_m.addr} size={_m.size()}",
                  file=_sys.stderr)

    def rs_base(par, si, chb):
        return ((par * NSTR + si) * NCHB + chb) * GW

    def z0_base(par, zi, h):
        return ((par * 3 + zi) * H + h) * GW

    sz0 = k.emit("V", lambda e: e.memset(S0_st[:, :], 0.0), deps=a5_done)
    mst = None
    for hp_ in range(4):
        for j_ in range(1, 6):
            w_ = 128 - (1 << j_)
            mst = k.emit("V", lambda e, hp_=hp_, j_=j_, w_=w_: e.memset(
                m_X[:, (hp_ * 6 + j_) * 128 + w_:(hp_ * 6 + j_ + 1) * 128],
                0.0), deps=a5_done if mst is None else [mst])
        for j_ in range(1, 7):
            o_ = 1 << j_
            mst = k.emit("V", lambda e, hp_=hp_, j_=j_, o_=o_: e.memset(
                m_Y[:, (hp_ * 7 + j_) * 128:(hp_ * 7 + j_) * 128 + o_],
                0.0), deps=[mst])
    clc = k.emit("S", lambda e: e.dma_start(cL0[:, :], cL_s[64:128, :]),
                 deps=a5_done, dma="LD")
    stok = [sz0] * H
    bank_free = {i: None for i in range(8)}
    bank_free[4] = mst
    bank_free[5] = mst
    mo_busy = [None] * H
    mo_wait = []

    def load_group(gi, deps):
        g0, gn = R_GROUPS[gi]
        par = gi % 2
        ctr = "LD" if par == 0 else "LD2"
        toks = []
        for si in range(NSTR):
            if si == 6 and g0 + gn <= NWARM:
                continue
            o0 = g0 * L
            on = gn * L
            if si == 6:
                o0 = max(g0, NWARM) * L
                on = (g0 + gn) * L - o0
            for chb in range(NCHB):
                toks.append(k.emit("S", lambda e, si=si, c=chb, o0=o0, on=on,
                                   g0=g0, par=par: e.dma_start(
                    rs[:, rs_base(par, si, c) + o0 - g0 * L:
                       rs_base(par, si, c) + o0 - g0 * L + on],
                    STRS[si][c][:, o0 - (W if si == 6 else 0):
                                o0 - (W if si == 6 else 0) + on]),
                    deps=deps, dma=ctr))
        for h in range(H):
            chb, half = h // 2, (h % 2) * 64
            toks.append(k.emit("S", lambda e, h=h, chb=chb, half=half, g0=g0,
                               gn=gn, par=par: e.dma_start(
                z0[:, z0_base(par, 0, h): z0_base(par, 0, h) + gn * L],
                Af_h[chb][half:half + 64, g0 * L: g0 * L + gn * L]),
                deps=deps, dma=ctr))
            if g0 + gn > NWARM:
                o0 = max(g0, NWARM) * L
                on = (g0 + gn) * L - o0
                toks.append(k.emit("S", lambda e, h=h, chb=chb, half=half,
                                   g0=g0, o0=o0, on=on, par=par: e.dma_start(
                    z0[:, z0_base(par, 1, h) + o0 - g0 * L:
                       z0_base(par, 1, h) + o0 - g0 * L + on],
                    r_h[chb][half:half + 64, o0 - W: o0 - W + on]),
                    deps=deps, dma=ctr))
                toks.append(k.emit("S", lambda e, h=h, chb=chb, half=half,
                                   g0=g0, o0=o0, on=on, par=par: e.dma_start(
                    z0[:, z0_base(par, 2, h) + o0 - g0 * L:
                       z0_base(par, 2, h) + o0 - g0 * L + on],
                    cfm_h[chb][half:half + 64, o0 - W: o0 - W + on]),
                    deps=deps, dma=ctr))
        return toks

    gtoks = [load_group(0, a5_done)]
    r1_hit = False
    for gi, (g0, gn) in enumerate(R_GROUPS):
        if r1_hit:
            break
        par = gi % 2
        pref_deps = k.engtok("T", "V") + mo_wait
        gl = gtoks[gi][-1]
        mo_wait = []
        for c in range(g0, g0 + gn):
            if r1_hit:
                break
            if c == g0 + 1 and gi + 1 < len(R_GROUPS):
                gtoks.append(load_group(gi + 1, pref_deps))
            co = (c - g0) * L
            warm = c < NWARM
            idn0 = cs["idn64"][0:64, :]

            def MX(hp, j):
                return m_X[:, (hp * 6 + j) * 128:(hp * 6 + j + 1) * 128]

            def MYs(hp, j):
                return m_Y[:, (hp * 7 + j) * 128:(hp * 7 + j + 1) * 128]

            def emit_grams(h, co=co, warm=warm, gl=gl, par=par):
                hp = h % 4
                chb, half = h // 2, (h % 2) * 64

                def fsl(si):
                    b = rs_base(par, si, chb) + co
                    return rs[half:half + 64, b: b + L]
                Af_, Kc_, Vc_ = fsl(0), fsl(1), fsl(2)
                RK_, R_ = fsl(5), fsl(6)
                idn = cs["idn64"][half:half + 64, :]
                k.emit("T", lambda e, Af_=Af_, Kc_=Kc_: e.matmul(
                    pb[0][:, 0:128], Af_, Kc_, start=True, stop=False),
                    deps=[gl, bank_free[0]])
                k.emit("T", lambda e, Af_=Af_, Kc_=Kc_: e.matmul(
                    pb[0][:, 128:256], Kc_, Af_, start=False, stop=False))
                glast = k.emit("T", lambda e, Af_=Af_, Vc_=Vc_: e.matmul(
                    pb[0][:, 256:384], Vc_, Af_, start=False, stop=warm))
                if not warm:
                    glast = k.emit("T", lambda e, RK_=RK_, R_=R_: e.matmul(
                        pb[0][:, 384:512], RK_, R_, start=False, stop=True))
                evX = k.emit("V", lambda e, hp=hp: e.tensor_tensor(
                    MX(hp, 0), pb[0][:, 0:128], cs["mSLn"][:, :], Alu.mult),
                    deps=[glast])
                evY = k.emit("V", lambda e, hp=hp: e.tensor_tensor(
                    MYs(hp, 0), pb[0][:, 128:256], cs["mSUn"][:, :], Alu.mult),
                    deps=[glast])
                evM = k.emit("V", lambda e, hp=hp: e.tensor_tensor(
                    m_MvT[:, hp * 128:hp * 128 + 128], pb[0][:, 256:384],
                    cs["mSU"][:, :], Alu.mult), deps=[glast])
                evP = None
                if not warm:
                    evP = k.emit("V", lambda e, hp=hp: e.tensor_tensor(
                        m_Prk[:, hp * 128:hp * 128 + 128], pb[0][:, 384:512],
                        cs["mUI"][:, :], Alu.mult), deps=[glast])
                bank_free[0] = evP or evM

                tsrc = [fsl(5), fsl(3), fsl(4)] + ([] if warm else
                                                   [fsl(2), fsl(1)])
                tmm = None
                for i, src in enumerate(tsrc):
                    tmm = k.emit("T", lambda e, i=i, src=src, idn=idn,
                                 st=(i == 0), sp=(i == len(tsrc) - 1): e.matmul(
                        pb[1][:, i * 64:(i + 1) * 64], src, idn,
                        start=st, stop=sp),
                        deps=[gl, bank_free[1]] if i == 0 else [])
                evT = k.emit("A", lambda e, hp=hp, n=len(tsrc): e.activation(
                    m_tm[:, hp * 320: hp * 320 + n * 64], pb[1][:, 0:n * 64],
                    Act.Copy, bias=0.0, scale=1.0), deps=[tmm])
                bank_free[1] = evT
                return dict(evX=evX, evY=evY, evM=evM, evP=evP, evT=evT)

            Gs = [emit_grams(0), emit_grams(1)] if not r1_hit else None
            for h0 in range(0, H, 2):
                if r1_hit:
                    break
                Gpair = Gs
                if h0 + 2 < H:
                    Gs = [emit_grams(h0 + 2), emit_grams(h0 + 3)]

                def TM(i, hp):
                    return m_tm[:, hp * 320 + i * 64: hp * 320 + (i + 1) * 64]

                vstate = []
                for hs in (0, 1):
                    h = h0 + hs
                    hp = h % 4
                    Af0_ = z0[:, z0_base(par, 0, h) + co:
                              z0_base(par, 0, h) + co + L]
                    Ssl = S0_st[:, h * 64: h * 64 + 64]
                    G = Gpair[hs]
                    b1 = k.emit("T", lambda e, Af0_=Af0_, Ssl=Ssl, hs=hs:
                                e.matmul(
                        pb[3][:, hs * 64: hs * 64 + 64], Af0_, Ssl,
                        start=(hs == 0), stop=False),
                        deps=[gl, stok[h], bank_free[3]] if hs == 0
                        else [stok[h]])
                    vti = k.emit("T", lambda e, hp=hp, hs=hs, t0s=TM(0, hp):
                                 e.matmul(
                        pb[3][:, hs * 64: hs * 64 + 64],
                        m_MvT[:, hp * 128:hp * 128 + 128], t0s,
                        start=False, stop=False), deps=[G["evM"], G["evT"]])
                    vstate.append(dict(b1=b1, xtok={0: G["evX"]},
                                       ytok={0: G["evY"]}))
                vtok = vti
                for j in range(7):
                    evV = k.emit("A", lambda e: e.activation(
                        m_V[:, :], pb[3][:, 0:128], Act.Copy, bias=0.0,
                        scale=1.0), deps=[vtok])
                    for hs in (0, 1):
                        h = h0 + hs
                        hp = h % 4
                        vs = vstate[hs]
                        vtok = k.emit("T", lambda e, j=j, hp=hp, hs=hs:
                                      e.matmul(
                            pb[3][:, hs * 64: hs * 64 + 64], MYs(hp, j),
                            m_V[:, hs * 64: hs * 64 + 64],
                            start=False, stop=(j == 6 and hs == 1)),
                            deps=[evV, vs["ytok"][j]])
                    if j < 6:
                        for hs in (0, 1):
                            h = h0 + hs
                            hp = h % 4
                            vs = vstate[hs]
                            bk = 4 + hs
                            off = 1 << (j + 1)
                            wdt = 128 - off
                            mmy = k.emit("T", lambda e, j=j, bk=bk, hp=hp,
                                         off=off: e.matmul(
                                pb[bk][:, off:128], MX(hp, j),
                                MYs(hp, j)[:, off:128],
                                start=True, stop=(j == 5)),
                                deps=[vs["xtok"][j], vs["ytok"][j],
                                      bank_free[bk]])
                            mmx = None
                            if j < 5:
                                mmx = k.emit("T", lambda e, j=j, bk=bk, hp=hp,
                                             wdt=wdt: e.matmul(
                                    pb[bk][:, 128:128 + wdt], MYs(hp, j),
                                    MX(hp, j)[:, 0:wdt],
                                    start=False, stop=True))
                            ee = "A" if hs == 0 else "V"
                            if ee == "A":
                                evy = k.emit("A", lambda e, j=j, bk=bk, hp=hp,
                                             off=off: e.activation(
                                    MYs(hp, j + 1)[:, off:128],
                                    pb[bk][:, off:128],
                                    Act.Copy, bias=0.0, scale=1.0),
                                    deps=[mmx or mmy])
                            else:
                                evy = k.emit("V", lambda e, j=j, bk=bk, hp=hp,
                                             off=off: e.tensor_copy(
                                    MYs(hp, j + 1)[:, off:128],
                                    pb[bk][:, off:128]),
                                    deps=[mmx or mmy])
                            vs["ytok"][j + 1] = evy
                            if j < 5:
                                if ee == "A":
                                    evx = k.emit("A", lambda e, j=j, bk=bk,
                                                 hp=hp, wdt=wdt: e.activation(
                                        MX(hp, j + 1)[:, 0:wdt],
                                        pb[bk][:, 128:128 + wdt],
                                        Act.Copy, bias=0.0, scale=1.0),
                                        deps=[mmx])
                                else:
                                    evx = k.emit("V", lambda e, j=j, bk=bk,
                                                 hp=hp, wdt=wdt: e.tensor_copy(
                                        MX(hp, j + 1)[:, 0:wdt],
                                        pb[bk][:, 128:128 + wdt]),
                                        deps=[mmx])
                                vs["xtok"][j + 1] = evx
                                bank_free[bk] = evx
                            else:
                                bank_free[bk] = evy
                evU = k.emit("A", lambda e: e.activation(
                    m_U[:, :], pb[3][:, 0:128], Act.Copy, bias=0.0, scale=1.0),
                    deps=[vtok])
                bank_free[3] = evU

                for hs in (0, 1):
                    h = h0 + hs
                    hp = h % 4
                    chb, half = h // 2, (h % 2) * 64
                    warmh = warm
                    G = Gpair[hs]
                    evP, evT = G["evP"], G["evT"]
                    r0_ = z0[:, z0_base(par, 1, h) + co:
                             z0_base(par, 1, h) + co + L]
                    cf0_ = z0[:, z0_base(par, 2, h) + co:
                              z0_base(par, 2, h) + co + L]
                    Ssl = S0_st[:, h * 64: h * 64 + 64]
                    if half == 0:
                        cLc = cL_s[0:64, chb * NCH + c: chb * NCH + c + 1]
                    else:
                        cLc = cL0[:, chb * NCH + c: chb * NCH + c + 1]
                    z1 = None
                    if not warmh:
                        z1 = k.emit("T", lambda e, Ssl=Ssl, idn0=idn0:
                                    e.matmul(
                            pb[2][0:64, 0:64], Ssl, idn0,
                            start=True, stop=False),
                            deps=[stok[h], bank_free[2]])
                        z2 = k.emit("T", lambda e, hs=hs: e.matmul(
                            pb[2][0:64, 64:192],
                            m_U[:, hs * 64: hs * 64 + 64],
                            cs["idn128"][:, :],
                            start=False, stop=True), deps=[evU])
                        evZ = k.emit("A", lambda e: e.activation(
                            m_Z[:, :], pb[2][0:64, 0:64], Act.Copy, bias=0.0,
                            scale=1.0), deps=[z2])
                        evUf = k.emit("A", lambda e: e.activation(
                            m_Ufm[:, :], pb[2][0:64, 64:192], Act.Copy,
                            bias=0.0, scale=1.0), deps=[z2])
                        z3 = k.emit("T", lambda e, r0_=r0_: e.matmul(
                            pb[2][:, 192:320], m_Ufm[:, :], r0_,
                            start=True, stop=True), deps=[evUf, gl])
                        evPu = k.emit("V", lambda e: e.tensor_tensor(
                            m_Put[:, :], pb[2][:, 192:320], cs["mUIn"][:, :],
                            Alu.mult), deps=[z3])
                        bank_free[2] = evPu
                        o1 = k.emit("T", lambda e, r0_=r0_: e.matmul(
                            pb[6][0:64, 0:128], m_Z[:, :], r0_, start=True,
                            stop=False), deps=[evZ, bank_free[6]])
                        o2 = k.emit("T", lambda e, hp=hp, t3s=TM(3, hp):
                                    e.matmul(
                            pb[6][0:64, 0:128], t3s,
                            m_Prk[:, hp * 128:hp * 128 + 128], start=False,
                            stop=False), deps=[evP, evT])
                        o3 = k.emit("T", lambda e, t4s=TM(4, hp): e.matmul(
                            pb[6][0:64, 0:128], t4s, m_Put[:, :], start=False,
                            stop=True), deps=[evPu])
                        mo_dst = m_mo[:, h * GW + co: h * GW + co + L]
                        evO = k.emit("V", lambda e, mo_dst=mo_dst, cf0_=cf0_:
                                     e.tensor_tensor(mo_dst,
                                                     pb[6][0:64, 0:128],
                                                     cf0_, Alu.mult),
                                     deps=[o3, mo_busy[h], gl])
                        bank_free[6] = evO
                        mo_busy[h] = evO
                    s1 = k.emit("T", lambda e, t1s=TM(1, hp), t0s=TM(0, hp):
                                e.matmul(
                        pb[7][0:64, 0:64], t1s, t0s, start=True, stop=False),
                        deps=[evT, bank_free[7]])
                    s2 = k.emit("T", lambda e, hs=hs, t2s=TM(2, hp): e.matmul(
                        pb[7][0:64, 0:64], t2s,
                        m_U[:, hs * 64: hs * 64 + 64],
                        start=False, stop=True), deps=[evU])
                    cd = k.emit("V", lambda e, Ssl=Ssl, cLc=cLc:
                                e.tensor_scalar(
                        m_cd[:, :], Ssl, cLc, None, Alu.mult),
                        deps=[stok[h], s1, clc])
                    sn = k.emit("V", lambda e, Ssl=Ssl: e.tensor_tensor(
                        Ssl, pb[7][0:64, 0:64], m_cd[:, :], Alu.add),
                        deps=[s2, cd, vstate[hs]["b1"], z1])
                    stok[h] = sn
                    bank_free[7] = sn
                if STOP_AFTER == "R1" and c == R1_C and h == R1_H:
                    r1_hit = True
                    dsc = sbr("dsc", [128, 1024], F32)
                    allb = k.barrier()
                    pd = [None]

                    def rdump(nm, sl, P, Wd):
                        t = nc.dram_tensor("dR_" + nm, [P, Wd], F32,
                                           kind="ExternalOutput")
                        cv = k.emit("V", lambda e, sl=sl, P=P, Wd=Wd:
                                    e.tensor_copy(dsc[0:P, 0:Wd], sl),
                                    deps=allb + [pd[0]])
                        pd[0] = k.emit("G", lambda e, t=t, P=P, Wd=Wd:
                                       e.dma_start(t[:, :], dsc[0:P, 0:Wd]),
                                       deps=[cv], dma="SD")

                    for si, nm in enumerate(["Af", "Kc", "Vc", "Wv", "Wkn",
                                             "RK", "R"]):
                        b = rs_base(par, si, chb) + co
                        rdump(nm, rs[half:half + 64, b: b + L], 64, L)
                    rdump("Af0", z0[:, z0_base(par, 0, h) + co:
                                    z0_base(par, 0, h) + co + L], 64, L)
                    rdump("r0", z0[:, z0_base(par, 1, h) + co:
                                   z0_base(par, 1, h) + co + L], 64, L)
                    rdump("cf0", z0[:, z0_base(par, 2, h) + co:
                                    z0_base(par, 2, h) + co + L], 64, L)
                    rdump("X", m_X[:, hp * 768: hp * 768 + 768],
                          128, 6 * 128)
                    rdump("Y", m_Y[:, hp * 896: hp * 896 + 896],
                          128, 7 * 128)
                    rdump("MvT", m_MvT[:, hp * 128: hp * 128 + 128], 128, 128)
                    rdump("Prk", m_Prk[:, hp * 128: hp * 128 + 128], 128, 128)
                    rdump("tm", m_tm[:, hp * 320: hp * 320 + 320], 128, 320)
                    rdump("U", m_U[:, :], 128, 64)
                    rdump("Ufm", m_Ufm[:, :], 64, 128)
                    rdump("Z", m_Z[:, :], 64, 64)
                    rdump("Put", m_Put[:, :], 128, 128)
                    rdump("cd", m_cd[:, :], 64, 64)
                    rdump("Sn", S0_st[:, h * 64: h * 64 + 64], 64, 64)
                    rdump("mo", m_mo[:, h * GW + co: h * GW + co + L], 64, L)
                    rdump("cLs", cL_s[:, :], 128, NCHB * NCH)
                    rdump("cL0", cL0[:, :], 64, NCHB * NCH)
                    break
        if g0 + gn > NWARM and not r1_hit:
            o0 = max(g0, NWARM) * L
            on = (g0 + gn) * L - o0
            for h in range(H):
                chb, half = h // 2, (h % 2) * 64
                mo_busy[h] = k.emit("G", lambda e, h=h, chb=chb, half=half,
                                    o0=o0, on=on, g0=g0: e.dma_start(
                    mo_h[chb][half:half + 64, o0 - W: o0 - W + on],
                    m_mo[:, h * GW + o0 - g0 * L: h * GW + o0 - g0 * L + on]),
                    deps=[mo_busy[h]], dma="SD")
                mo_wait.append(mo_busy[h])
    r_done = k.barrier()
    stackR.close()

    if STOP_AFTER in ("R", "R1"):
        if STOP_AFTER == "R":
            emit_debug(nc, k, dbg)
        k.realize()
        stack.close()
        return nc

    build_phase_p(nc, k, pb, cs, pcols, sc_s, ec_s, wt, mo_h, bon_h, gate_h,
                  xT, out_t, r_done, sb)

    emit_debug(nc, k, dbg)
    k.realize()
    stack.close()
    return nc


def emit_debug(nc, k, dbg):
    if not DEBUG:
        return
    with nc.sbuf_tensor(uname("dbf"), [128, TT], F32) as dbf, \
         nc.sbuf_tensor(uname("dbb"), [128, TT], BF16) as dbb:
        import sys as _sys
        for _nm, _t in (("dbf", dbf), ("dbb", dbb)):
            _m = nc.lookup_mloc(_t)
            print(f"DALLOC {_nm}: addr={_m.addr} size={_m.size()}",
                  file=_sys.stderr)
        dd = k.barrier()
        prev = None
        for nm, (ot, st) in dbg.items():
            wdt = st[0].shape[1]
            isf = st[0].dtype == F32
            for chb in range(NCHB):
                l1 = k.emit("S", lambda e, st=st, c=chb, wdt=wdt, isf=isf:
                            e.dma_start(dbf[:, 0:wdt] if isf else dbb[:, 0:wdt],
                                        st[c][:, :]),
                            deps=dd + [prev], dma="LD")
                cv = k.emit("V", lambda e, wdt=wdt, isf=isf: e.tensor_copy(
                    dbf[:, 0:wdt], dbf[:, 0:wdt] if isf else dbb[:, 0:wdt]),
                    deps=[l1])
                prev = k.emit("G", lambda e, ot=ot, c=chb, wdt=wdt:
                              e.dma_start(bass.AP(ot, c * 128 * wdt,
                                                  [[wdt, 128], [1, wdt]]),
                                          dbf[:, 0:wdt]), deps=[cv], dma="SD")


def build_phase_p(nc, k, pb, cs, pcols, sc_s, ec_s, wt, mo_h, bon_h, gate_h,
                  xT, out_t, r_done, sb):
    invs = sb("invs", [128, 1], F32)
    scol = sb("scol", [128, 1], F32)
    sb16 = sb("sb16", [1, 1], BF16)
    t0_ = k.emit("V", lambda e: e.tensor_scalar(
        sc_s[:, :], sc_s[:, :], 1e-5, None, Alu.max), deps=r_done)
    t1_ = k.emit("V", lambda e: e.tensor_copy(sb16[:, :], sc_s[:, :]), deps=[t0_])
    t1_ = k.emit("V", lambda e: e.tensor_copy(sb16[:, :], sc_s[:, :]), deps=[t1_])
    t2_ = k.emit("T", lambda e: e.matmul(
        pb[5][:, 0:1], cs["ones128"][0:1, :], sb16[:, :], start=True, stop=True),
        deps=[t1_] + r_done)
    t3_ = k.emit("V", lambda e: e.tensor_copy(scol[:, :], pb[5][:, 0:1]),
                 deps=[t2_])
    t3_ = k.emit("V", lambda e: e.tensor_copy(scol[:, :], pb[5][:, 0:1]),
                 deps=[t3_])
    t4_ = k.emit("V", lambda e: e.reciprocal(invs[:, :], scol[:, :]), deps=[t3_])
    t4_ = k.emit("V", lambda e: e.reciprocal(invs[:, :], scol[:, :]), deps=[t4_])

    ow1b = sb("ow1b", [128, NCHB * EP], BF16)
    ow2b = sb("ow2b", [128, NEB * C], BF16)
    fwb = sb("fwb", [128, NCHB * C], BF16)
    ob1c = sb("ob1c", [128, NEB], F32)
    with nc.sbuf_tensor(uname("pwld"), [128, EP], F32) as pwld:
        ltok = None
        wt1 = []
        for kc in range(NCHB):
            ld = k.emit("S", lambda e, kc=kc: e.dma_start(
                pwld[:, 0:EP], bass.AP(wt["oW1"], kc * 128 * EP,
                                       [[EP, 128], [1, EP]])),
                deps=r_done + [ltok], dma="LD")
            ltok = k.emit("V", lambda e, kc=kc: e.tensor_copy(
                ow1b[:, kc * EP: kc * EP + EP], pwld[:, 0:EP]), deps=[ld])
            wt1.append(ltok)
        for ec in range(NEB):
            ld = k.emit("S", lambda e, ec=ec: e.dma_start(
                pwld[:, 0:C], bass.AP(wt["oW2"], ec * 128 * C, [[C, 128], [1, C]])),
                deps=[ltok], dma="LD")
            ltok = k.emit("V", lambda e, ec=ec: e.tensor_copy(
                ow2b[:, ec * C: ec * C + C], pwld[:, 0:C]), deps=[ld])
            wt1.append(ltok)
        for kc in range(NCHB):
            ld = k.emit("S", lambda e, kc=kc: e.dma_start(
                pwld[:, 0:C], bass.AP(wt["fWT"], kc * 128 * C, [[C, 128], [1, C]])),
                deps=[ltok], dma="LD")
            ltok = k.emit("V", lambda e, kc=kc: e.tensor_scalar(
                fwb[:, kc * C: kc * C + C], pwld[:, 0:C], -1.0, None, Alu.add),
                deps=[ld])
            wt1.append(ltok)
        cb = k.emit("V", lambda e: e.tensor_copy(ob1c[:, :], pcols["ob1"][:, :]),
                    deps=r_done)
        wdone = ltok

    sq_b = sb("p_sq", [128, NCHB * 512], BF16)
    ot_b = sb("p_ot", [128, NCHB * 512], BF16)
    o2_b = sb("p_o2", [128, NCHB * 512], BF16)
    hb_b = sb("p_hb", [128, NEB * 512], BF16)
    x2_b = sb("p_x2", [128, NCHB * 512], F32)
    y_b = sb("p_y", [128, NCHB * 512], BF16)
    stm = sb("p_stm", [128, 512], F32)
    strr = sb("p_str", [128, 512], F32)
    tq1 = sb("p_tq1", [128, 512], F32)
    tq2 = sb("p_tq2", [128, 512], F32)
    tq3 = sb("p_tq3", [128, 512], F32)
    tq4 = sb("p_tq4", [128, 512], F32)
    tqi = sb("p_tqi", [128, 512], I32)

    if DEBUG:
        import sys as _sys
        for _nm, _t in (("sq_b", sq_b), ("ot_b", ot_b), ("o2_b", o2_b),
                        ("hb_b", hb_b), ("x2_b", x2_b), ("y_b", y_b),
                        ("stm", stm), ("strr", strr), ("tq1", tq1),
                        ("tq2", tq2), ("tq3", tq3), ("tq4", tq4),
                        ("tqi", tqi), ("ow1b", ow1b), ("ow2b", ow2b),
                        ("fwb", fwb), ("invs", invs), ("scol", scol)):
            _m = nc.lookup_mloc(_t)
            print(f"PALLOC {_nm}: addr={_m.addr} size={_m.size()}",
                  file=_sys.stderr)

    psfree = {i: None for i in range(8)}

    def ln_stats(src_fn, eps, deps, tl):
        sqs = []
        for chb in range(NCHB):
            sqs.append(k.emit("V", lambda e, c=chb, tl=tl: e.tensor_mul(
                sq_b[:, c * 512: c * 512 + tl], src_fn(c), src_fn(c)),
                deps=list(deps)))
        mm1 = mm2 = None
        for chb in range(NCHB):
            mm1 = k.emit("T", lambda e, c=chb, tl=tl: e.matmul(
                pb[6][:, 0:tl], cs["ones128"][:, :], src_fn(c),
                start=(c == 0), stop=(c == NCHB - 1)),
                deps=list(deps) + [psfree[6]])
        for chb in range(NCHB):
            mm2 = k.emit("T", lambda e, c=chb, tl=tl: e.matmul(
                pb[7][:, 0:tl], cs["ones128"][:, :],
                sq_b[:, c * 512: c * 512 + tl],
                start=(c == 0), stop=(c == NCHB - 1)),
                deps=[sqs[chb], psfree[7]])
        e1 = k.emit("V", lambda e, tl=tl: e.tensor_scalar(
            stm[:, 0:tl], pb[6][:, 0:tl], 1.0 / C, None, Alu.mult), deps=[mm1])
        e2 = k.emit("V", lambda e, tl=tl: e.tensor_mul(
            strr[:, 0:tl], stm[:, 0:tl], stm[:, 0:tl]), deps=[e1])
        e3 = k.emit("V", lambda e, tl=tl: e.scalar_tensor_tensor(
            strr[:, 0:tl], pb[7][:, 0:tl], 1.0 / C, strr[:, 0:tl],
            Alu.mult, Alu.subtract), deps=[mm2, e2])
        e4 = k.emit("A", lambda e, tl=tl, eps=eps: e.activation(
            strr[:, 0:tl], strr[:, 0:tl], Act.Ln, bias=(ec_s[:, 1:2] if eps == 1e-5 else ec_s[:, 2:3]), scale=1.0), deps=[e3])
        e5 = k.emit("A", lambda e, tl=tl: e.activation(
            strr[:, 0:tl], strr[:, 0:tl], Act.Exp, bias=ec_s[:, 0:1], scale=-0.5),
            deps=[e4])
        psfree[6] = e1
        psfree[7] = e3
        return e5

    pdbg = {}
    if DEBUG:
        for nm, dt_ in (("pg", BF16), ("pm", BF16), ("px", F32),
                        ("pq", BF16), ("pf", BF16), ("pa", BF16)):
            pdbg[nm] = nc.dram_tensor("dbg_" + nm, [C, TC], dt_,
                                      kind="ExternalOutput")
        for nm in ("ps", "pr"):
            pdbg[nm] = nc.dram_tensor("dbg_" + nm, [128, TC], F32,
                                      kind="ExternalOutput")
        pdbg["pv"] = nc.dram_tensor("dbg_pv", [128, 4], F32,
                                    kind="ExternalOutput")
        pdbg["pz"] = nc.dram_tensor("dbg_pz", [128, 512], BF16,
                                    kind="ExternalOutput")
        k.emit("G", lambda e: e.dma_start(
            pdbg["pv"][:, 0:1], scol[:, 0:1],
            allow_slow_non_contiguous=True), deps=[t4_], dma="SD")
        k.emit("G", lambda e: e.dma_start(
            pdbg["pv"][:, 1:2], invs[:, 0:1],
            allow_slow_non_contiguous=True), deps=[t4_], dma="SD")

    def pdump(nm, buf, chb, t0, tl, dep):
        if not DEBUG:
            return dep
        return k.emit("G", lambda e, nm=nm, c=chb, t0=t0, tl=tl, buf=buf:
                      e.dma_start(bass.AP(pdbg[nm], c * 128 * TC + (t0 - W),
                                          [[TC, 128], [1, tl]]),
                                  buf[:, c * 512: c * 512 + tl]),
                      deps=[dep], dma="SD")

    prev_tile = None
    for (t0, tl) in P_TILES:
        base = [wdone, t4_, cb, prev_tile]
        lts, gts = [], []
        for chb in range(NCHB):
            l1 = k.emit("S", lambda e, c=chb, t0=t0, tl=tl: e.dma_start(
                ot_b[:, c * 512: c * 512 + tl], mo_h[c][:, t0 - W: t0 - W + tl]),
                deps=base, dma="LD")
            l1 = k.emit("S", lambda e, c=chb, t0=t0, tl=tl: e.dma_start(
                ot_b[:, c * 512: c * 512 + tl], mo_h[c][:, t0 - W: t0 - W + tl]),
                deps=[l1], dma="LD")
            l2 = k.emit("S", lambda e, c=chb, t0=t0, tl=tl: e.dma_start(
                sq_b[:, c * 512: c * 512 + tl], bon_h[c][:, t0 - W: t0 - W + tl]),
                deps=base, dma="LD")
            l2 = k.emit("S", lambda e, c=chb, t0=t0, tl=tl: e.dma_start(
                sq_b[:, c * 512: c * 512 + tl], bon_h[c][:, t0 - W: t0 - W + tl]),
                deps=[l2], dma="LD")
            lts.append(pdump("pa", ot_b, chb, t0, tl, k.emit(
                "V", lambda e, c=chb, tl=tl: e.tensor_add(
                    ot_b[:, c * 512: c * 512 + tl],
                    ot_b[:, c * 512: c * 512 + tl],
                    sq_b[:, c * 512: c * 512 + tl]),
                deps=[l1, l2, prev_tile])))
            g1_ = k.emit("S", lambda e, c=chb, t0=t0, tl=tl: e.dma_start(
                o2_b[:, c * 512: c * 512 + tl],
                gate_h[c][:, t0 - W: t0 - W + tl]), deps=base, dma="LD")
            gts.append(k.emit("S", lambda e, c=chb, t0=t0, tl=tl: e.dma_start(
                o2_b[:, c * 512: c * 512 + tl],
                gate_h[c][:, t0 - W: t0 - W + tl]), deps=[g1_], dma="LD"))
        r5 = ln_stats(lambda c: ot_b[:, c * 512: c * 512 + tl], 1e-6, lts, tl)
        if DEBUG and t0 == P_TILES[0][0]:
            with nc.sbuf_tensor(uname("diagz"), [128, 512], BF16) as diag_b:
                zl = k.emit("S", lambda e: e.dma_start(
                    diag_b[:, :], mo_h[4][:, 0:512]), deps=[r5], dma="LD")
                r5 = k.emit("G", lambda e: e.dma_start(
                    pdbg["pz"][:, :], diag_b[:, :]), deps=[zl], dma="SD")
        if DEBUG:
            r5 = k.emit("G", lambda e, t0=t0, tl=tl: e.dma_start(
                pdbg["pr"][:, t0 - W: t0 - W + tl], strr[:, 0:tl]),
                deps=[r5], dma="SD")
            r5 = k.emit("G", lambda e, t0=t0, tl=tl: e.dma_start(
                pdbg["ps"][:, t0 - W: t0 - W + tl], stm[:, 0:tl]),
                deps=[r5], dma="SD")
        o2t = []
        for chb in range(NCHB):
            s1 = k.emit("V", lambda e, c=chb, tl=tl: e.tensor_sub(
                ot_b[:, c * 512: c * 512 + tl], ot_b[:, c * 512: c * 512 + tl],
                stm[:, 0:tl]), deps=[r5])
            s2 = k.emit("V", lambda e, c=chb, tl=tl: e.tensor_mul(
                ot_b[:, c * 512: c * 512 + tl], ot_b[:, c * 512: c * 512 + tl],
                strr[:, 0:tl]), deps=[s1])
            s3 = k.emit("V", lambda e, c=chb, tl=tl: e.tensor_scalar(
                ot_b[:, c * 512: c * 512 + tl], ot_b[:, c * 512: c * 512 + tl],
                pcols["ln_g"][:, c: c + 1], pcols["ln_b"][:, c: c + 1],
                Alu.mult, Alu.add), deps=[s2])
            o2t.append(pdump("pg", o2_b, chb, t0, tl, k.emit(
                "V", lambda e, c=chb, tl=tl: e.tensor_mul(
                    o2_b[:, c * 512: c * 512 + tl],
                    o2_b[:, c * 512: c * 512 + tl],
                    ot_b[:, c * 512: c * 512 + tl]), deps=[s3, gts[chb]])))
        htok = [None] * NEB
        for ec in range(NEB):
            bk = ec % 2
            mmt = None
            for kc in range(NCHB):
                mmt = k.emit("T", lambda e, kc=kc, ec=ec, tl=tl, bk=bk,
                             st=(kc == 0): e.matmul(
                    pb[bk][:, 0:tl],
                    ow1b[:, kc * EP + ec * 128: kc * EP + ec * 128 + 128],
                    o2_b[:, kc * 512: kc * 512 + tl],
                    start=st, stop=(kc == NCHB - 1)),
                    deps=[o2t[kc], psfree.get(bk)])
            htok[ec] = k.emit("A", lambda e, ec=ec, tl=tl, bk=bk: e.activation(
                hb_b[:, ec * 512: ec * 512 + tl], pb[bk][:, 0:tl], Act.Gelu,
                bias=ob1c[:, ec: ec + 1], scale=1.0), deps=[mmt])
            psfree[bk] = htok[ec]
        ootok = []
        for cc in range(NCHB):
            bk = 2 + cc % 2
            mmt = None
            for ec in range(NEB):
                mmt = k.emit("T", lambda e, ec=ec, cc=cc, tl=tl, bk=bk,
                             st=(ec == 0): e.matmul(
                    pb[bk][:, 0:tl],
                    ow2b[:, ec * C + cc * 128: ec * C + cc * 128 + 128],
                    hb_b[:, ec * 512: ec * 512 + tl],
                    start=st, stop=(ec == NEB - 1)),
                    deps=[htok[ec], psfree.get(bk)])
            ev = k.emit("V", lambda e, cc=cc, tl=tl, bk=bk: e.tensor_scalar(
                ot_b[:, cc * 512: cc * 512 + tl], pb[bk][:, 0:tl],
                pcols["ob2"][:, cc: cc + 1], None, Alu.add), deps=[mmt])
            psfree[bk] = ev
            ootok.append(pdump("pm", ot_b, cc, t0, tl, ev))
        r5 = ln_stats(lambda c: ot_b[:, c * 512: c * 512 + tl], 1e-5, ootok, tl)
        x2t = []
        for chb in range(NCHB):
            lx = k.emit("S", lambda e, c=chb, t0=t0, tl=tl: e.dma_start(
                x2_b[:, c * 512: c * 512 + tl],
                bass.AP(xT, c * 128 * TT + t0, [[TT, 128], [1, tl]])),
                deps=base, dma="LD")
            lx = k.emit("S", lambda e, c=chb, t0=t0, tl=tl: e.dma_start(
                x2_b[:, c * 512: c * 512 + tl],
                bass.AP(xT, c * 128 * TT + t0, [[TT, 128], [1, tl]])),
                deps=[lx], dma="LD")
            s1 = k.emit("V", lambda e, c=chb, tl=tl: e.tensor_sub(
                tq1[:, 0:tl], ot_b[:, c * 512: c * 512 + tl], stm[:, 0:tl]),
                deps=[r5, lx])
            s2 = k.emit("V", lambda e, tl=tl: e.tensor_mul(
                tq1[:, 0:tl], tq1[:, 0:tl], strr[:, 0:tl]), deps=[s1])
            s3 = k.emit("V", lambda e, c=chb, tl=tl: e.tensor_add(
                x2_b[:, c * 512: c * 512 + tl], x2_b[:, c * 512: c * 512 + tl],
                tq1[:, 0:tl]), deps=[s2])
            x2t.append(pdump("px", x2_b, chb, t0, tl, k.emit(
                "V", lambda e, c=chb, tl=tl: e.tensor_copy(
                    y_b[:, c * 512: c * 512 + tl],
                    x2_b[:, c * 512: c * 512 + tl]), deps=[s3])))
        r5 = ln_stats(lambda c: y_b[:, c * 512: c * 512 + tl], 1e-5, x2t, tl)
        qt = []
        for chb in range(NCHB):
            q1 = k.emit("V", lambda e, c=chb, tl=tl: e.tensor_sub(
                tq1[:, 0:tl], x2_b[:, c * 512: c * 512 + tl], stm[:, 0:tl]),
                deps=[r5])
            q2 = k.emit("V", lambda e, tl=tl: e.tensor_mul(
                tq1[:, 0:tl], tq1[:, 0:tl], strr[:, 0:tl]), deps=[q1])
            q3 = k.emit("V", lambda e, tl=tl: e.tensor_scalar(
                tq1[:, 0:tl], tq1[:, 0:tl], invs[:, 0:1], None, Alu.mult),
                deps=[q2])
            q4 = k.emit("V", lambda e, tl=tl: e.tensor_scalar(
                tq1[:, 0:tl], tq1[:, 0:tl], -1000.0, 1000.0, Alu.max, Alu.min),
                deps=[q3])
            q5 = k.emit("V", lambda e, tl=tl: e.tensor_copy(
                tqi[:, 0:tl], tq1[:, 0:tl]), deps=[q4])
            q6 = k.emit("V", lambda e, tl=tl: e.tensor_copy(
                tq2[:, 0:tl], tqi[:, 0:tl]), deps=[q5])
            q7 = k.emit("V", lambda e, tl=tl: e.tensor_sub(
                tq3[:, 0:tl], tq1[:, 0:tl], tq2[:, 0:tl]), deps=[q6])
            q8 = k.emit("V", lambda e, tl=tl: e.tensor_scalar(
                tq4[:, 0:tl], tq3[:, 0:tl], 0.5, None, Alu.is_gt), deps=[q7])
            q9 = k.emit("V", lambda e, tl=tl: e.tensor_add(
                tq2[:, 0:tl], tq2[:, 0:tl], tq4[:, 0:tl]), deps=[q8])
            qa = k.emit("V", lambda e, tl=tl: e.tensor_scalar(
                tq4[:, 0:tl], tq3[:, 0:tl], -0.5, None, Alu.is_lt), deps=[q9])
            qb = k.emit("V", lambda e, tl=tl: e.tensor_sub(
                tq2[:, 0:tl], tq2[:, 0:tl], tq4[:, 0:tl]), deps=[qa])
            qt.append(pdump("pq", y_b, chb, t0, tl, k.emit(
                "V", lambda e, c=chb, tl=tl: e.tensor_scalar(
                    y_b[:, c * 512: c * 512 + tl], tq2[:, 0:tl], -127.0, 127.0,
                    Alu.max, Alu.min), deps=[qb])))
        ftok = []
        for cc in range(NCHB):
            bk = 4 + cc % 2
            mmt = None
            for kc in range(NCHB):
                mmt = k.emit("T", lambda e, kc=kc, cc=cc, tl=tl, bk=bk,
                             st=(kc == 0): e.matmul(
                    pb[bk][:, 0:tl],
                    fwb[:, kc * C + cc * 128: kc * C + cc * 128 + 128],
                    y_b[:, kc * 512: kc * 512 + tl],
                    start=st, stop=(kc == NCHB - 1)),
                    deps=[qt[kc], psfree.get(bk)])
            ev = k.emit("V", lambda e, cc=cc, tl=tl, bk=bk: e.tensor_scalar(
                ot_b[:, cc * 512: cc * 512 + tl], pb[bk][:, 0:tl],
                scol[:, 0:1], pcols["fbias"][:, cc: cc + 1], Alu.mult, Alu.add),
                deps=[mmt])
            psfree[bk] = ev
            ftok.append(pdump("pf", ot_b, cc, t0, tl, ev))
        r5 = ln_stats(lambda c: ot_b[:, c * 512: c * 512 + tl], 1e-5, ftok, tl)
        for chb in range(NCHB):
            s1 = k.emit("V", lambda e, c=chb, tl=tl: e.tensor_sub(
                tq1[:, 0:tl], ot_b[:, c * 512: c * 512 + tl], stm[:, 0:tl]),
                deps=[r5])
            s2 = k.emit("V", lambda e, tl=tl: e.tensor_mul(
                tq1[:, 0:tl], tq1[:, 0:tl], strr[:, 0:tl]), deps=[s1])
            s3 = k.emit("V", lambda e, c=chb, tl=tl: e.tensor_add(
                x2_b[:, c * 512: c * 512 + tl], x2_b[:, c * 512: c * 512 + tl],
                tq1[:, 0:tl]), deps=[s2])
            s3 = k.emit("V", lambda e, c=chb, tl=tl: e.tensor_copy(
                tq4[:, 0:tl], x2_b[:, c * 512: c * 512 + tl]), deps=[s3])
            prev_tile = k.emit("G", lambda e, c=chb, t0=t0, tl=tl: e.dma_start(
                bass.AP(out_t, c * 128 * TC + (t0 - W), [[TC, 128], [1, tl]]),
                x2_b[:, c * 512: c * 512 + tl]), deps=[s3], dma="SD")


_NC_CACHE = {}


def kernel(**inputs):
    p = "L0_attn_"
    x = np.asarray(inputs[p + "queries"], np.float32)[0]
    bf = ml_dtypes.bfloat16

    def padE(a, axis):
        pad = [(0, 0), (0, 0)]
        pad[axis] = (0, EP - E)
        return np.ascontiguousarray(np.pad(np.asarray(a, np.float32), pad))

    def padEv(v):
        return np.pad(np.asarray(v, np.float32), (0, EP - E))

    com = {
        "rW1": padE(inputs[p + "rW1"], 1), "rW2": padE(inputs[p + "rW2"], 0),
        "kW1": padE(inputs[p + "kW1"], 1), "kW2": padE(inputs[p + "kW2"], 0),
        "vW1": padE(inputs[p + "vW1"], 1), "vW2": padE(inputs[p + "vW2"], 0),
        "oW1": padE(inputs[p + "oW1"], 1), "oW2": padE(inputs[p + "oW2"], 0),
        "gA": np.asarray(inputs[p + "gA"], np.float32),
        "gB": np.asarray(inputs[p + "gB"], np.float32),
        "aA": np.asarray(inputs[p + "aA"], np.float32),
        "aB": np.asarray(inputs[p + "aB"], np.float32),
        "dA": np.asarray(inputs[p + "dA"], np.float32),
        "dB": np.asarray(inputs[p + "dB"], np.float32),
        "fWT": np.ascontiguousarray(
            np.asarray(inputs["L0_ffn_ternary_weight"]).astype(np.float32).T),
        "rb1": padEv(inputs[p + "rb1"]), "kb1": padEv(inputs[p + "kb1"]),
        "vb1": padEv(inputs[p + "vb1"]), "ob1": padEv(inputs[p + "ob1"]),
        "act_scale": np.asarray(inputs["L0_ffn_act_scale"],
                                np.float32).reshape(1, 1),
        "fbias": np.asarray(inputs["L0_ffn_bias"], np.float32),
    }
    for n in ["mu_r", "mu_k", "mu_v", "mu_g", "mu_a", "mu_d", "rb2", "kb2",
              "vb2", "ob2", "ab", "db", "rem_mult", "iclr_mix", "bonus_mult",
              "ln_g", "ln_b"]:
        com[n] = np.asarray(inputs[p + n], np.float32)
    ec = np.zeros((128, 4), np.float32)
    ec[:, 1] = 1e-5
    ec[:, 2] = 1e-6
    com["epscol"] = ec
    idn2 = np.concatenate([np.eye(64), np.eye(64)], 0)
    com["idn64"] = idn2.astype(bf)
    com["idn128"] = np.eye(128, dtype=bf)
    sl = np.tril(np.ones((128, 128), np.float32), -1)
    com["mSLn"] = -sl
    com["mSUn"] = np.ascontiguousarray(-sl.T)
    com["mSU"] = np.ascontiguousarray(sl.T)
    ui = np.triu(np.ones((128, 128), np.float32), 0)
    com["mUIn"] = -ui
    com["mUI"] = ui
    bones = np.zeros((128, 128), np.float32)
    bones[:64, :64] = 1.0
    bones[64:, 64:] = 1.0
    com["bones"] = bones.astype(bf)
    com["ones128"] = np.ones((128, 128), dtype=bf)

    if "nc" not in _NC_CACHE:
        _NC_CACHE["nc"] = build_nc()
    nc = _NC_CACHE["nc"]

    in_maps = []
    for i in range(NCORES):
        if i == 0:
            xs = np.concatenate([np.zeros((W, C), np.float32), x[0:TC]], 0)
            wm = np.zeros((128, W), np.float32)
        else:
            xs = x[i * TC - W: i * TC + TC]
            wm = np.ones((128, W), np.float32)
        m = dict(com)
        m["xT"] = np.ascontiguousarray(xs.T)
        m["wmask"] = wm
        in_maps.append(m)

    res = run_bass_kernel_spmd(nc, in_maps, core_ids=list(range(NCORES)))
    shards = [np.ascontiguousarray(res.results[i]["out"].T)
              for i in range(NCORES)]
    out = np.concatenate(shards, 0)[None]
    if DEBUG:
        kernel._dbg = res.results
    return out
